# version 01
# baseline (speedup 1.0000x reference)
import numpy as np
import concourse.bass as bass
import concourse.mybir as mybir
from concourse.bass_utils import run_bass_kernel_spmd

F32 = mybir.dt.float32
U8 = mybir.dt.uint8
AF = mybir.ActivationFunctionType
ALU = mybir.AluOpType
AX = mybir.AxisListType

B, T, D = 8, 2048, 1024
NTILE = T // 128
NCHUNK = D // 128
BIGC = 16448
HEAP_COLS = 52430


COMPUTE_ENGINES = ("tensor", "vector", "scalar", "gpsimd")


class Fw:

    def __init__(self, nc, n_dma_sems=80):
        self.nc = nc
        self.sems = {}
        self.cnt = {}
        for e in COMPUTE_ENGINES:
            self.sems[e] = nc.semaphore(f"fws_{e}").__enter__()
            self.cnt[e] = 0
        self.dma_keys = {"hw": [], "sw": []}
        for i in range(n_dma_sems):
            pool = "sw" if i < n_dma_sems // 4 else "hw"
            key = f"dma_{pool}{i}"
            self.sems[key] = nc.semaphore(f"fwd_{i}").__enter__()
            self.cnt[key] = 0
            self.dma_keys[pool].append(key)
        self._dma_rr = {"hw": 0, "sw": 0}
        self.hwm = {}
        self.wr = {}
        self.rd = {}
        self.frozen = set()
        self.nops = 0

    def eng(self, name):
        return getattr(self.nc, name)

    def group(self, queue="sync"):
        pool = "sw" if queue == "gpsimd" else "hw"
        keys = self.dma_keys[pool]
        key = keys[self._dma_rr[pool] % len(keys)]
        self._dma_rr[pool] += 1
        return key

    def _gather(self, ins, outs, extra):
        toks = []
        for k in ins:
            t = self.wr.get(k)
            if t is not None:
                toks.append(t)
        for k in outs:
            t = self.wr.get(k)
            if t is not None:
                toks.append(t)
            toks.extend(self.rd.get(k, ()))
        toks.extend(t for t in extra if t is not None)
        return toks

    def _wait(self, waiter, toks):
        merged = {}
        for sk, c in toks:
            if sk.startswith("dma"):
                c = self.cnt[sk]
                self.frozen.add(sk)
            if c > 0 and merged.get(sk, 0) < c:
                merged[sk] = c
        for sk, c in sorted(merged.items()):
            if self.hwm.get((waiter, sk), 0) >= c:
                continue
            self.eng(waiter).wait_ge(self.sems[sk], c)
            self.hwm[(waiter, sk)] = c

    def _record(self, tok, ins, outs):
        for k in ins:
            self.rd.setdefault(k, []).append(tok)
        for k in outs:
            self.wr[k] = tok
            self.rd[k] = []

    def emit(self, engine, method, *args, ins=(), outs=(), extra=(), **kw):
        self._wait(engine, self._gather(ins, outs, extra))
        inst = getattr(self.eng(engine), method)(*args, **kw)
        self.cnt[engine] += 1
        inst.then_inc(self.sems[engine], 1)
        self.nops += 1
        tok = (engine, self.cnt[engine])
        self._record(tok, ins, outs)
        return tok

    def dma(self, queue, grp, *, out, in_, ins=(), outs=(), extra=(), allow_slow=False):
        if grp in self.frozen:
            c = self.cnt[grp]
            if self.hwm.get((queue, grp), 0) < c:
                self.eng(queue).wait_ge(self.sems[grp], c)
                self.hwm[(queue, grp)] = c
            self.frozen.discard(grp)
        self._wait(queue, self._gather(ins, outs, extra))
        if allow_slow:
            inst = self.eng(queue).dma_start(out=out, in_=in_, allow_slow_non_contiguous=True)
        else:
            inst = self.eng(queue).dma_start(out=out, in_=in_)
        self.cnt[grp] += 16
        inst.then_inc(self.sems[grp], 16)
        self.nops += 1
        tok = (grp, self.cnt[grp])
        self._record(tok, ins, outs)
        return tok

    def mm_group(self, psum_key, items, ins=(), n_banks=1):
        n = len(items)
        for i, (o, l, r) in enumerate(items):
            self.emit("tensor", "matmul", o, l, r, start=(i == 0), stop=(i == n - 1),
                      ins=ins, outs=([psum_key] if i == 0 else []),
                      extra=([] if i == 0 else [self.wr.get(psum_key)]))
        self.wr[psum_key] = ("tensor", self.cnt["tensor"])

    def alias(self, new_key, old_keys):
        toks = []
        for k in old_keys:
            t = self.wr.get(k)
            if t is not None:
                toks.append(t)
            toks.extend(self.rd.get(k, ()))
        if toks:
            self.rd[new_key] = self.rd.get(new_key, []) + toks

    def finish(self, waiter="gpsimd"):
        toks = [(k, v) for k, v in self.cnt.items() if v > 0]
        self._wait(waiter, toks)


class Heap:

    def __init__(self, nc, fw, cols=HEAP_COLS, name="heap"):
        self.ap = nc.sbuf_tensor(name, [128, cols], F32).__enter__()
        self.fw = fw
        self.cols = cols
        self.top = 0
        self.live = {}
        self.subs = {}
        self.holes = []

    def _merge(self):
        self.holes.sort(key=lambda h: h[0])
        out = []
        for h in self.holes:
            if out and out[-1][1] == h[0]:
                out[-1][1] = h[1]
                out[-1][2] = out[-1][2] + h[2]
            else:
                out.append(h)
        while out and out[-1][1] >= self.top:
            h = out.pop()
            self.top = h[0]
            self.top_keys = getattr(self, "top_keys", []) + h[2]
        self.holes = out

    def alloc(self, name, ncols, align=2, subs=()):
        ncols = (ncols + align - 1) // align * align
        self._merge()
        best = None
        for h in self.holes:
            s = (h[0] + align - 1) // align * align
            room = h[1] - s
            if room >= ncols and (best is None or (h[1] - h[0]) < (best[1] - best[0])):
                best = h
        if best is not None:
            h = best
            if ncols < 1024 and (h[1] - ncols) % align == 0:
                start, end = h[1] - ncols, h[1]
                h[1] = start
            else:
                start = (h[0] + align - 1) // align * align
                end = start + ncols
                h[0] = end
            self.fw.alias(name, h[2])
            for sub in subs:
                self.fw.alias(sub, h[2])
            if h[1] - h[0] <= 0:
                self.holes.remove(h)
            self.live[name] = (start, end)
            self.subs[name] = list(subs)
            return self.ap[:, start:end]
        start = (self.top + align - 1) // align * align
        end = start + ncols
        if end > self.cols:
            import sys
            print("HEAP STATE live:", sorted(self.live.items(), key=lambda kv: kv[1][0]), file=sys.stderr)
            print("holes:", [(h[0], h[1]) for h in self.holes], "top:", self.top, file=sys.stderr)
        assert end <= self.cols, f"heap overflow: {name} needs [{start},{end}) cap {self.cols}"
        self.top = end
        if getattr(self, "top_keys", None):
            self.fw.alias(name, self.top_keys)
            for sub in subs:
                self.fw.alias(sub, self.top_keys)
        self.live[name] = (start, end)
        self.subs[name] = list(subs)
        return self.ap[:, start:end]

    def free(self, name):
        s, e = self.live.pop(name)
        self.holes.append([s, e, [name] + self.subs.pop(name, [])])

    def view(self, name):
        s, e = self.live[name]
        return self.ap[:, s:e]



def _rope_tables_l0(T_, Dh=128, theta=10000.0):
    inv = 1.0 / theta ** (np.arange(0, Dh, 2, dtype=np.float64) / Dh)
    ang = np.arange(T_, dtype=np.float64)[:, None] * inv[None, :]
    cos = np.cos(ang)
    sin = np.sin(ang)
    cosf = np.concatenate([cos, cos], axis=1).T
    sinf = np.concatenate([-sin, sin], axis=1).T
    return cosf.astype(np.float32), sinf.astype(np.float32)


def _xpos_tables(T_, Dh=64, window=512):
    half_idx = np.arange(0, Dh, 2, dtype=np.float64)
    inv = 1.0 / 10000.0 ** (half_idx / Dh)
    t = np.arange(T_, dtype=np.float64)
    ang = t[:, None] * inv[None, :]
    ang = np.concatenate([ang, ang], axis=1)
    base = (half_idx + 0.4 * Dh) / (1.4 * Dh)
    power = (t - T_ // 2) / float(window // 2)
    scale = base[None, :] ** power[:, None]
    scale = np.concatenate([scale, scale], axis=1)
    sign = np.concatenate([-np.ones(32), np.ones(32)])[None, :]
    tabs = {
        "xq_cos": np.cos(ang) * scale,
        "xq_sin": np.sin(ang) * scale * sign,
        "xk_cos": np.cos(ang) / scale,
        "xk_sin": np.sin(ang) / scale * sign,
    }
    out = {}
    for k, v in tabs.items():
        out[k] = np.ascontiguousarray(
            v.reshape(NTILE, 128, 64).transpose(1, 0, 2).reshape(128, NTILE * 64).astype(np.float32))
    return out


def _l0f_masks():
    r = np.arange(128)[:, None]
    c = np.arange(640)[None, :]
    parts = [((c < r) | (c > r + 512)).astype(np.uint8)]
    for t_ in range(4):
        w = 128 * (t_ + 1)
        cc = np.arange(w)[None, :]
        parts.append((cc > 128 * t_ + r).astype(np.uint8))
    c5 = np.arange(512)[None, :]
    parts.append(((c5 < r) | (c5 > r + 256)).astype(np.uint8))
    parts.append(((c5 < r + 128) | (c5 > r + 384)).astype(np.uint8))
    return np.concatenate(parts, axis=1)


def host_constants():
    c = {}
    c["ident"] = np.eye(128, dtype=np.float32)
    p64 = np.zeros((128, 128), np.float32)
    for i in range(128):
        p64[(i + 64) % 128, i] = 1.0
    c["perm64"] = p64
    cos0, sin0 = _rope_tables_l0(T)
    c["cos0"] = cos0
    c["sin0"] = sin0
    c["triu_u8"] = np.triu(np.ones((128, 128), np.uint8), k=1)
    c["ones1"] = np.ones((1, 128), np.float32)
    c.update(_xpos_tables(T))
    c["l0f_masks"] = _l0f_masks()
    return c


def _prep_l0_ffn(full, shared):
    f32 = lambda a: np.ascontiguousarray(a, dtype=np.float32)
    wqkv = f32(full["L0_ffn_wqkv"])
    shared["f_wqkv"] = np.ascontiguousarray(
        wqkv.reshape(2, 8, 128, 3, 512).transpose(0, 3, 2, 1, 4))
    shared["f_wo"] = f32(full["L0_ffn_wo"])
    shared["f_lng"] = np.ascontiguousarray(f32(full["L0_ffn_ln_g"]).reshape(2, 8, 128).transpose(0, 2, 1))
    shared["f_lnb"] = np.ascontiguousarray(f32(full["L0_ffn_ln_b"]).reshape(2, 8, 128).transpose(0, 2, 1))
    shared["f_qs"] = f32(full["L0_ffn_q_scale"])
    shared["f_ks"] = f32(full["L0_ffn_k_scale"])
    w1 = f32(full["L0_ffn_w1"])
    b1 = f32(full["L0_ffn_b1"])
    w1p = np.zeros((2, 22, 128, 8, 2, 128), np.float32)
    b1c = np.zeros((2, 128, 44), np.float32)
    for i in range(22):
        w = min(128, 2730 - 128 * i)
        for h2 in range(2):
            sl = w1[:, :, 2730 * h2 + 128 * i: 2730 * h2 + 128 * i + w]
            w1p[:, i, :, :, h2, :w] = sl.reshape(2, 8, 128, w).transpose(0, 2, 1, 3)
            b1c[:, :w, 22 * h2 + i] = b1[:, 2730 * h2 + 128 * i: 2730 * h2 + 128 * i + w]
    shared["f_w1p"] = np.ascontiguousarray(w1p)
    shared["f_b1c"] = b1c
    w2 = f32(full["L0_ffn_w2"])
    w2p = np.zeros((2, 22, 128, 1024), np.float32)
    for i in range(22):
        w = min(128, 2730 - 128 * i)
        w2p[:, i, :w, :] = w2[:, 128 * i:128 * i + w, :]
    shared["f_w2p"] = np.ascontiguousarray(w2p)
    shared["f_b2"] = f32(full["L0_ffn_b2"])


def shard_inputs(full):
    consts = host_constants()
    f32 = lambda a: np.ascontiguousarray(a, dtype=np.float32)
    x = f32(full["L0_attn_hidden_states"])
    wq = f32(full["L0_attn_Wq"])
    wk = f32(full["L0_attn_Wk"])
    wv = f32(full["L0_attn_Wv"])
    shared = dict(consts)
    shared["a_wq"] = np.ascontiguousarray(wq.reshape(8, 128, 8, 128).transpose(2, 1, 0, 3))
    shared["a_wk"] = np.ascontiguousarray(wk.reshape(8, 128, 4, 128).transpose(2, 1, 0, 3))
    shared["a_wv"] = np.ascontiguousarray(wv.reshape(8, 128, 4, 128).transpose(2, 1, 0, 3))
    shared["a_wo"] = f32(full["L0_attn_Wo"])
    _prep_l0_ffn(full, shared)
    wq1 = f32(full["L1_attn_Wq"]).reshape(8, 128, 1024)
    wk1 = f32(full["L1_attn_Wk"]).reshape(8, 128, 1024)
    shared["g_wqk"] = np.ascontiguousarray(np.concatenate([wq1, wk1], axis=2))
    shared["g_wv"] = np.ascontiguousarray(f32(full["L1_attn_Wv"]).reshape(8, 128, 1024))
    shared["g_bqk"] = np.ascontiguousarray(np.concatenate(
        [f32(full["L1_attn_bq"]).reshape(8, 128).T, f32(full["L1_attn_bk"]).reshape(8, 128).T], axis=1))
    shared["g_bv"] = f32(full["L1_attn_bv"]).reshape(1, 1024)
    shared["h_w1T"] = np.ascontiguousarray(f32(full["L1_ffn_w1"]).T)
    shared["h_w2T"] = np.ascontiguousarray(f32(full["L1_ffn_w2"]).T)
    cw = f32(full["L1_ffn_conv_w"])[:, 0, :]
    shared["h_ccol"] = np.ascontiguousarray(cw.reshape(32, 128, 3).transpose(1, 2, 0).reshape(128, 96))
    shared["h_bcol"] = np.ascontiguousarray(f32(full["L1_ffn_conv_b"]).reshape(32, 128).T)
    in_maps = []
    for b in range(B):
        m = dict(shared)
        m["x_in"] = f32(x[b])
        m["xT_in"] = np.ascontiguousarray(x[b].T)
        in_maps.append(m)
    return in_maps



class Ctx:
    def __init__(self, nc, dump=()):
        self.nc = nc
        self.fw = Fw(nc)
        self.heap = Heap(nc, self.fw)
        self.dump = set(dump)
        self.dram = {}
        self.psum = []
        for i in range(4):
            self.psum.append(nc.psum_tensor(f"pb{i}", [128, 512], F32).__enter__())
        self.pbig = nc.psum_tensor("pbig", [128, 2048], F32).__enter__()
        self.tri = nc.sbuf_tensor("tri_u8", [128, 128], U8).__enter__()
        self.masks = nc.sbuf_tensor("masks_u8", [128, 2944], U8).__enter__()

    def din(self, name, shape, dtype=F32):
        if name not in self.dram:
            self.dram[name] = self.nc.dram_tensor(name, list(shape), dtype, kind="ExternalInput").ap()
        return self.dram[name]

    def dout(self, name, shape, dtype=F32):
        self.dram[name] = self.nc.dram_tensor(name, list(shape), dtype, kind="ExternalOutput").ap()
        return self.dram[name]

    def dump_raw(self, name, src_ap, keys, rows=128):
        if name not in self.dump:
            return
        cols = src_ap.shape[-1]
        o = self.dout("dump_" + name, [rows, cols])
        g = self.fw.group("gpsimd")
        self.fw.dma("gpsimd", g, out=o[:, :], in_=src_ap, ins=list(keys))

    def maybe_dump(self, key, name=None):
        if key not in self.dump:
            return
        fw, hp = self.fw, self.heap
        o = self.dout("dump_" + (name or key), [T, D])
        g = fw.group("gpsimd")
        src = hp.view(key)
        for t in range(NTILE):
            fw.dma("gpsimd", g, out=o[128 * t:128 * (t + 1), :],
                   in_=src[:, 1024 * t:1024 * (t + 1)], ins=[key])


def load_constants(ctx):
    fw, hp = ctx.fw, ctx.heap
    g = fw.group()
    ident = hp.alloc("ident", 128)
    perm64 = hp.alloc("perm64", 128)
    negbig = hp.alloc("negbig", 640)
    fw.dma("sync", g, out=ident, in_=ctx.din("ident", [128, 128]), outs=["ident"])
    fw.dma("sync", g, out=perm64, in_=ctx.din("perm64", [128, 128]), outs=["perm64"])
    fw.dma("sync", g, out=ctx.tri[:], in_=ctx.din("triu_u8", [128, 128], U8), outs=["triu"])
    fw.emit("gpsimd", "memset", negbig, -1.0e37, outs=["negbig"])
    eps = hp.alloc("epscol", 2)
    fw.emit("gpsimd", "memset", eps, 1e-5, outs=["epscol"])
    onesr = hp.alloc("ones1", 128)
    fw.dma("sync", g, out=onesr[0:1, :], in_=ctx.din("ones1", [1, 128]), outs=["ones1"])
    fw.dma("sync", g, out=ctx.masks[:], in_=ctx.din("l0f_masks", [128, 2944], U8), outs=["masks"])


def postln(ctx, y_key, out_key, resid_key=None):
    fw, hp = ctx.fw, ctx.heap
    y = hp.view(y_key)
    out = hp.view(out_key)
    nt = NTILE
    st = hp.alloc("ln_st", 4 * nt, subs=("ln_s1", "ln_sq", "ln_mu", "ln_rs"))
    s1, sq = st[:, 0:nt], st[:, nt:2 * nt]
    mu, rs = st[:, 2 * nt:3 * nt], st[:, 3 * nt:4 * nt]
    fw.alias("PBIG0", [])
    for t in range(nt):
        yt = y[:, 1024 * t:1024 * (t + 1)]
        fw.emit("vector", "reduce_sum", s1[:, t:t + 1], yt, axis=AX.X, ins=[y_key], outs=["ln_s1"])
        fw.emit("scalar", "activation", ctx.pbig[:, 0:1024], yt, AF.Square,
                accum_out=sq[:, t:t + 1], ins=[y_key], outs=["PBIG0", "PBIG1", "ln_sq"])
    inv = 1.0 / 1024.0
    fw.emit("vector", "tensor_scalar", mu, s1, inv, None, op0=ALU.mult, ins=["ln_s1"], outs=["ln_mu"])
    fw.emit("vector", "tensor_scalar", sq, sq, inv, None, op0=ALU.mult, ins=["ln_sq"], outs=["ln_sq"])
    fw.emit("vector", "tensor_tensor", rs, mu, mu, op=ALU.mult, ins=["ln_mu"], outs=["ln_rs"])
    fw.emit("vector", "tensor_tensor", rs, sq, rs, op=ALU.subtract, ins=["ln_sq", "ln_rs"], outs=["ln_rs"])
    fw.emit("scalar", "activation", rs, rs, AF.Sqrt, bias=hp.view("epscol")[:, 0:1], scale=1.0,
            ins=["ln_rs", "epscol"], outs=["ln_rs"])
    fw.emit("vector", "reciprocal", rs, rs, ins=["ln_rs"], outs=["ln_rs"])
    for t in range(nt):
        fw.emit("vector", "tensor_scalar", out[:, 1024 * t:1024 * (t + 1)], y[:, 1024 * t:1024 * (t + 1)],
                mu[:, t:t + 1], rs[:, t:t + 1], op0=ALU.subtract, op1=ALU.mult,
                ins=[y_key, "ln_mu", "ln_rs"], outs=[out_key])
    hp.free("ln_st")


def rope_l0(ctx, raw_key, out_key, pT, nblocks=4):
    fw, hp = ctx.fw, ctx.heap
    raw = hp.view(raw_key)
    out = hp.view(out_key)
    swp = hp.view("swp")
    for b in range(nblocks):
        sl = slice(512 * b, 512 * (b + 1))
        fw.emit("tensor", "matmul", pT[:], hp.view("perm64"), raw[:, sl],
                start=True, stop=True, ins=["perm64", raw_key], outs=["PB1"])
        fw.emit("vector", "tensor_tensor", swp, pT[:], hp.view("sin0")[:, sl], op=ALU.mult,
                ins=["PB1", "sin0"], outs=["swp"])
        fw.emit("vector", "tensor_tensor", out[:, sl], raw[:, sl], hp.view("cos0")[:, sl], op=ALU.mult,
                ins=[raw_key, "cos0"], outs=[out_key])
        fw.emit("vector", "tensor_tensor", out[:, sl], out[:, sl], swp, op=ALU.add,
                ins=["swp", out_key], outs=[out_key])


def build_l0_attn(ctx):
    nc, fw, hp = ctx.nc, ctx.fw, ctx.heap
    NKV = 4

    xT = hp.alloc("xT", BIGC)
    aout = hp.alloc("aout", BIGC)
    cos0 = hp.alloc("cos0", 2048)
    sin0 = hp.alloc("sin0", 2048)
    kr = hp.alloc("kr", 2048)
    vv = hp.alloc("vv", 2048)
    qr = hp.alloc("qr", 2048)
    work = hp.alloc("work", 2560, subs=("kraw", "swp", "qraw"))
    attnT = hp.alloc("attnT", 256, subs=("attnT0", "attnT1"))
    wk = hp.alloc("wk", 1024)
    wvb = hp.alloc("wvb", 1024)
    wqb = hp.alloc("wqb", 1024)
    sm = hp.alloc("sm", 8, subs=("sm0", "sm1", "sm2", "sm3"))
    kraw = work[:, 0:2048]
    attn = work[:, 0:2048]
    ws = hp.live["work"][0]
    hp.live["kraw"] = (ws, ws + 2048)
    hp.live["swp"] = (ws + 2048, ws + 2560)

    gx = fw.group()
    for c in range(NCHUNK):
        fw.dma("sync", gx, out=xT[:, 2048 * c:2048 * (c + 1)],
               in_=ctx.din("xT_in", [D, T])[128 * c:128 * (c + 1), :], outs=["xT"])
    gt = fw.group()
    fw.dma("sync", gt, out=cos0, in_=ctx.din("cos0", [128, 2048]), outs=["cos0"])
    fw.dma("sync", gt, out=sin0, in_=ctx.din("sin0", [128, 2048]), outs=["sin0"])

    pK, pT, pO, pV = ctx.psum[0], ctx.psum[1], ctx.psum[2], ctx.psum[3]
    pS = ctx.pbig

    for g in range(NKV):
        gk = fw.group()
        fw.dma("sync", gk, out=wk, in_=ctx.din("a_wk", [4, 128, 8, 128])[g].rearrange("p c f -> p (c f)"), outs=["wk"])
        fw.dma("sync", gk, out=wvb, in_=ctx.din("a_wv", [4, 128, 8, 128])[g].rearrange("p c f -> p (c f)"), outs=["wv"])
        for b in range(4):
            fw.mm_group("PB0", [(pK[:], wk[:, 128 * c:128 * (c + 1)],
                                 xT[:, 2048 * c + 512 * b:2048 * c + 512 * (b + 1)]) for c in range(NCHUNK)],
                        ins=["wk", "xT"])
            fw.emit("vector", "tensor_copy", kraw[:, 512 * b:512 * (b + 1)], pK[:], ins=["PB0"], outs=["kraw"])
        rope_l0(ctx, "kraw", "kr", pT)
        for t in range(NTILE):
            fw.mm_group("PB3", [(pV[:, 0:128], xT[:, 2048 * c + 128 * t:2048 * c + 128 * (t + 1)],
                                 wvb[:, 128 * c:128 * (c + 1)]) for c in range(NCHUNK)],
                        ins=["wv", "xT"])
            fw.emit("vector", "tensor_copy", vv[:, 128 * t:128 * (t + 1)], pV[:, 0:128], ins=["PB3"], outs=["vv"])
        for h in (2 * g, 2 * g + 1):
            gq = fw.group()
            fw.dma("sync", gq, out=wqb, in_=ctx.din("a_wq", [8, 128, 8, 128])[h].rearrange("p c f -> p (c f)"), outs=["wq"])
            for b in range(4):
                fw.mm_group("PB0", [(pK[:], wqb[:, 128 * c:128 * (c + 1)],
                                     xT[:, 2048 * c + 512 * b:2048 * c + 512 * (b + 1)]) for c in range(NCHUNK)],
                            ins=["wq", "xT"])
                fw.emit("vector", "tensor_copy", kraw[:, 512 * b:512 * (b + 1)], pK[:], ins=["PB0"], outs=["kraw"])
            rope_l0(ctx, "kraw", "qr", pT)
            for t in range(NTILE):
                nk = 128 * (t + 1)
                nkb = (nk + 511) // 512
                pdeps = []
                for kb in range(nkb):
                    n = min(512, nk - 512 * kb)
                    fw.emit("tensor", "matmul", pS[:, 512 * kb:512 * kb + n],
                            qr[:, 128 * t:128 * (t + 1)], kr[:, 512 * kb:512 * kb + n],
                            start=True, stop=True, ins=["qr", "kr"], outs=[f"PBIG{kb}"])
                    pdeps.append(f"PBIG{kb}")
                dkey = f"PBIG{(nk - 128) // 512}"
                fw.emit("vector", "copy_predicated", pS[:, nk - 128:nk], ctx.tri[:],
                        hp.view("negbig")[:, 0:128], ins=["triu", "negbig", dkey], outs=[dkey])
                fw.emit("vector", "tensor_reduce", sm[:, 0:1], pS[:, 0:nk], axis=AX.X, op=ALU.max,
                        negate=True, ins=pdeps, outs=["sm0"])
                fw.emit("vector", "tensor_scalar", sm[:, 1:2], sm[:, 0:1], 0.08838834764831845, None,
                        op0=ALU.mult, ins=["sm0"], outs=["sm1"])
                fw.emit("scalar", "activation", attn[:, 0:nk], pS[:, 0:nk], AF.Exp,
                        bias=sm[:, 1:2], scale=0.08838834764831845, accum_out=sm[:, 2:3],
                        ins=pdeps + ["sm1"], outs=["kraw", "sm2"])
                fw.emit("vector", "reciprocal", sm[:, 3:4], sm[:, 2:3], ins=["sm2"], outs=["sm3"])
                fw.emit("vector", "tensor_scalar", attn[:, 0:nk], attn[:, 0:nk], sm[:, 3:4], None,
                        op0=ALU.mult, ins=["kraw", "sm3"], outs=["kraw"])
                first = True
                for kb in range(t + 1):
                    a_sl = attnT[:, 128 * (kb % 2):128 * (kb % 2 + 1)]
                    akey = f"attnT{kb % 2}"
                    fw.emit("tensor", "transpose", pT[:, 0:128], attn[:, 128 * kb:128 * (kb + 1)],
                            hp.view("ident"), ins=["kraw", "ident"], outs=["PB1"])
                    fw.emit("vector", "tensor_copy", a_sl, pT[:, 0:128], ins=["PB1"], outs=[akey])
                    fw.emit("tensor", "matmul", pO[:, 0:128], vv[:, 128 * kb:128 * (kb + 1)], a_sl,
                            start=first, stop=(kb == t), ins=["vv", akey],
                            outs=(["PB2"] if first else []), extra=([] if first else [fw.wr.get("PB2")]))
                    first = False
                fw.wr["PB2"] = ("tensor", fw.cnt["tensor"])
                fw.emit("vector", "tensor_copy", aout[:, 2048 * h + 128 * t:2048 * h + 128 * (t + 1)],
                        pO[:, 0:128], ins=["PB2"], outs=["aout"])

    for nm in ("kr", "vv", "qr", "work", "attnT", "wk", "wvb", "wqb", "sm", "cos0", "sin0"):
        hp.free(nm)
    hp.live.pop("swp", None)
    hp.live.pop("kraw", None)

    hp.free("xT")
    y = hp.alloc("y", BIGC)
    wo = hp.alloc("wof", 8192)
    xb = hp.alloc("xb", 2048, subs=("xb0", "xb1"))
    gw2 = fw.group()
    for c in range(NCHUNK):
        fw.dma("sync", gw2, out=wo[:, 1024 * c:1024 * (c + 1)],
               in_=ctx.din("a_wo", [D, D])[128 * c:128 * (c + 1), :], outs=["wof"])
    gxr = fw.group()
    for t in range(NTILE):
        xs = xb[:, 1024 * (t % 2):1024 * (t % 2 + 1)]
        xkey = f"xb{t % 2}"
        fw.dma("sync", gxr, out=xs, in_=ctx.din("x_in", [T, D])[128 * t:128 * (t + 1), :], outs=[xkey])
        for half in range(2):
            fw.mm_group(f"PBIG{half}",
                        [(ctx.pbig[:, 512 * half:512 * (half + 1)],
                          hp.view("aout")[:, 2048 * c + 128 * t:2048 * c + 128 * (t + 1)],
                          wo[:, 1024 * c + 512 * half:1024 * c + 512 * (half + 1)]) for c in range(NCHUNK)],
                        ins=["aout", "wof"])
            fw.emit("vector", "tensor_tensor",
                    y[:, 1024 * t + 512 * half:1024 * t + 512 * (half + 1)],
                    ctx.pbig[:, 512 * half:512 * (half + 1)],
                    xs[:, 512 * half:512 * (half + 1)], op=ALU.add,
                    ins=[f"PBIG{half}", xkey], outs=["y"])
    hp.free("aout")
    hp.free("perm64")
    x1 = hp.alloc("x", BIGC)
    postln(ctx, "y", "x")
    hp.free("wof")
    hp.free("xb")
    hp.free("y")
    ctx.maybe_dump("x", "x1")


def ap_bc(sl, reps, width):
    return bass.AP(sl.tensor, sl.offset, [list(sl.ap[0]), [0, reps], [1, width]])


def ap_swap32(sl, heads):
    return bass.AP(sl.tensor, sl.offset + 32, [list(sl.ap[0]), [64, heads], [-32, 2], [1, 32]])


def transpose_tm_fm(ctx, src_key, dst_key, dst_ap, shifted, src_tiles=NTILE,
                    evac=None):
    fw, hp = ctx.fw, ctx.heap
    CW = 2056 if shifted else 2048
    src = hp.view(src_key)
    if shifted:
        for c in range(4, 8):
            fw.emit("gpsimd", "memset", dst_ap[:, CW * c:CW * c + 1], 0.0, outs=[dst_key])
    for c in range(NCHUNK):
        off = 1 if (shifted and c >= 4) else 0
        for t in range(src_tiles):
            pb = ctx.psum[t % 2]
            pk = f"PB{t % 2}"
            fw.emit("tensor", "transpose", pb[:, 0:128], src[:, 1024 * t + 128 * c:1024 * t + 128 * (c + 1)],
                    hp.view("ident"), ins=[src_key, "ident"], outs=[pk])
            dsl = dst_ap[:, CW * c + off + 128 * t: CW * c + off + 128 * (t + 1)]
            if evac is None:
                fw.emit("vector", "tensor_copy", dsl, pb[:, 0:128], ins=[pk], outs=[dst_key])
            else:
                evac(c, pb[:, 0:128], dsl, pk, dst_key)


def spill_residual(ctx, dram_name):
    fw, hp = ctx.fw, ctx.heap
    if dram_name not in ctx.dram:
        ctx.dram[dram_name] = ctx.nc.dram_tensor(dram_name, [T, D], F32).ap()
    d = ctx.dram[dram_name]
    g = fw.group("gpsimd")
    src = hp.view("x")
    for t in range(NTILE):
        fw.dma("gpsimd", g, out=d[128 * t:128 * (t + 1), :], in_=src[:, 1024 * t:1024 * (t + 1)],
               ins=["x"], outs=[dram_name + f"_t{t}"])


def build_l0_ffn(ctx):
    nc, fw, hp = ctx.nc, ctx.fw, ctx.heap
    spill_residual(ctx, "spill_l0f")
    for l in range(2):
        _l0f_layer(ctx, l)
    yf = hp.alloc("yfin", BIGC)
    xb = hp.alloc("xbf", 2048, subs=("xbf0", "xbf1"))
    d = ctx.dram["spill_l0f"]
    g = fw.group()
    for t in range(NTILE):
        xs = xb[:, 1024 * (t % 2):1024 * (t % 2 + 1)]
        xkey = f"xbf{t % 2}"
        fw.dma("sync", g, out=xs, in_=d[128 * t:128 * (t + 1), :],
               ins=[f"spill_l0f_t{t}"], outs=[xkey])
        fw.emit("vector", "tensor_tensor", yf[:, 1024 * t:1024 * (t + 1)],
                hp.view("x")[:, 1024 * t:1024 * (t + 1)], xs, op=ALU.add,
                ins=["x", xkey], outs=["yfin"])
    postln(ctx, "yfin", "x")
    hp.free("yfin")
    hp.free("xbf")
    ctx.maybe_dump("x", "x2")


def _qk_tile(ctx, psum, pkey, A_key, B_key, t, out512, out_key, sskey):
    fw, hp = ctx.fw, ctx.heap
    st = hp.view(sskey)
    ss, rn = st[:, 0:8], st[:, 16:24]
    for h in range(8):
        fw.emit("scalar", "activation", ctx.psum[3][:, 0:64], psum[:, 64 * h:64 * (h + 1)], AF.Square,
                accum_out=ss[:, h:h + 1], ins=[pkey], outs=["PB3", f"{sskey}_ss"])
    fw.emit("scalar", "activation", st[:, 8:16], ss, AF.Sqrt, bias=0.0, scale=1.0,
            ins=[f"{sskey}_ss"], outs=[f"{sskey}_n"])
    fw.emit("vector", "tensor_scalar", st[:, 8:16], st[:, 8:16], 1e-12, None, op0=ALU.max,
            ins=[f"{sskey}_n"], outs=[f"{sskey}_n"])
    fw.emit("vector", "reciprocal", rn, st[:, 8:16], ins=[f"{sskey}_n"], outs=[f"{sskey}_r"])
    u = hp.view("w512")
    for h in range(8):
        fw.emit("vector", "tensor_scalar", u[:, 64 * h:64 * (h + 1)], psum[:, 64 * h:64 * (h + 1)],
                rn[:, h:h + 1], None, op0=ALU.mult, ins=[pkey, f"{sskey}_r"], outs=["w512"])
    A = hp.view(A_key)[:, 64 * t:64 * (t + 1)]
    Bt = hp.view(B_key)[:, 64 * t:64 * (t + 1)]
    t512 = hp.view("t512")
    t3 = t512.rearrange("p (h f) -> p h f", f=64)
    fw.emit("vector", "tensor_tensor", t3, ap_swap32(u, 8), ap_bc(Bt, 8, 64), op=ALU.mult,
            ins=["w512", B_key], outs=["t512"])
    o3 = out512.rearrange("p (h f) -> p h f", f=64)
    u3 = u.rearrange("p (h f) -> p h f", f=64)
    fw.emit("vector", "tensor_tensor", o3, u3, ap_bc(A, 8, 64), op=ALU.mult,
            ins=["w512", A_key], outs=[out_key])
    fw.emit("vector", "tensor_tensor", out512, out512, t512, op=ALU.add,
            ins=["t512", out_key], outs=[out_key])


def _l0f_layer(ctx, l):
    nc, fw, hp = ctx.nc, ctx.fw, ctx.heap
    CW = 2056
    spill_residual(ctx, "spill_tmp")
    hp.free("x")
    xTs = hp.alloc("xTs", 8 * CW)
    xrb = hp.alloc("xrb", 2048, subs=("xrb0", "xrb1"))
    for c in range(4, 8):
        fw.emit("gpsimd", "memset", xTs[:, CW * c:CW * c + 1], 0.0, outs=["xTs"])
    d0 = ctx.dram["spill_tmp"]
    gr = fw.group()
    for t in range(NTILE):
        xs = xrb[:, 1024 * (t % 2):1024 * (t % 2 + 1)]
        xkey = f"xrb{t % 2}"
        fw.dma("sync", gr, out=xs, in_=d0[128 * t:128 * (t + 1), :],
               ins=[f"spill_tmp_t{t}"], outs=[xkey])
        for c in range(NCHUNK):
            off = 1 if c >= 4 else 0
            pb = ctx.psum[c % 2]
            pk = f"PB{c % 2}"
            fw.emit("tensor", "transpose", pb[:, 0:128], xs[:, 128 * c:128 * (c + 1)],
                    hp.view("ident"), ins=[xkey, "ident"], outs=[pk])
            fw.emit("vector", "tensor_copy", xTs[:, CW * c + off + 128 * t:CW * c + off + 128 * (t + 1)],
                    pb[:, 0:128], ins=[pk], outs=["xTs"])
    hp.free("xrb")

    raw = hp.alloc("xraw", 4096)
    g = fw.group()
    for i, nm in enumerate(("xq_cos", "xq_sin", "xk_cos", "xk_sin")):
        fw.dma("sync", g, out=raw[:, 1024 * i:1024 * (i + 1)], in_=ctx.din(nm, [128, 1024]), outs=["xraw"])
    qsr = hp.alloc("qsr", 128)
    fw.dma("sync", g, out=qsr[0:1, 0:64], in_=ctx.din("f_qs", [2, 64])[l:l + 1, :], outs=["qsr"])
    fw.dma("sync", g, out=qsr[0:1, 64:128], in_=ctx.din("f_ks", [2, 64])[l:l + 1, :], outs=["qsr"])
    ABq = hp.alloc("ABq", 2048)
    ABk = hp.alloc("ABk", 2048)
    qb = hp.alloc("qb", 256)
    for j, (base, swap) in enumerate([(0, False), (0, True), (64, False), (64, True)]):
        if swap:
            rhs = bass.AP(qsr.tensor, qsr.offset + base + 32, [[qsr.ap[0][0], 1], [-32, 2], [1, 32]])
        else:
            rhs = qsr[0:1, base:base + 64]
        fw.emit("tensor", "matmul", ctx.psum[0][:, 0:64], hp.view("ones1")[0:1, :], rhs,
                start=True, stop=True, ins=["ones1", "qsr"], outs=["PB0"])
        fw.emit("vector", "tensor_copy", qb[:, 64 * j:64 * (j + 1)], ctx.psum[0][:, 0:64],
                ins=["PB0"], outs=["qb"])
    for j in range(4):
        dst = ABq if j < 2 else ABk
        dk = "ABq" if j < 2 else "ABk"
        a3 = dst[:, 1024 * (j % 2):1024 * (j % 2 + 1)].rearrange("p (t f) -> p t f", f=64)
        r3 = raw[:, 1024 * j:1024 * (j + 1)].rearrange("p (t f) -> p t f", f=64)
        fw.emit("vector", "tensor_tensor", a3, r3, ap_bc(qb[:, 64 * j:64 * (j + 1)], NTILE, 64),
                op=ALU.mult, ins=["xraw", "qb"], outs=[dk])
    hp.free("xraw")
    hp.free("qb")
    hp.free("qsr")
    for k, src, off in (("Aq", "ABq", 0), ("Bq", "ABq", 1024), ("Ak", "ABk", 0), ("Bk", "ABk", 1024)):
        hp.live[k] = (hp.live[src][0] + off, hp.live[src][0] + off + 1024)
        fw.alias(k, [src])
        fw.wr[k] = fw.wr.get(src)

    krT = hp.alloc("krT", 8192)
    vbuf = hp.alloc("vbuf", 8192)
    qrT = hp.alloc("qrT", 8192)
    wthird = hp.alloc("wthird", 4096)
    w512 = hp.alloc("w512", 512)
    t512 = hp.alloc("t512", 512)
    p512 = hp.alloc("p512", 512)
    sst = hp.alloc("sst", 24, subs=("sst_ss", "sst_n", "sst_r"))
    smf = hp.alloc("smf", 8, subs=("smf0", "smf1", "smf2", "smf3"))

    wq = ctx.din("f_wqkv", [2, 3, 128, 8, 512])

    def load_third(j):
        gg = fw.group()
        fw.dma("sync", gg, out=wthird, in_=wq[l, j].rearrange("p c f -> p (c f)"), outs=["wthird"])

    def qkv_tile(t):
        fw.mm_group("PB2", [(ctx.psum[2][:], xTs[:, CW * c + 128 * t:CW * c + 128 * (t + 1)],
                             wthird[:, 512 * c:512 * (c + 1)]) for c in range(NCHUNK)],
                    ins=["xTs", "wthird"])

    load_third(1)
    for t in range(NTILE):
        qkv_tile(t)
        _qk_tile(ctx, ctx.psum[2][:], "PB2", "Ak", "Bk", t, hp.view("p512"), "p512", "sst")
        for j in range(4):
            fw.emit("tensor", "transpose", ctx.psum[0][:, 0:128], hp.view("p512")[:, 128 * j:128 * (j + 1)],
                    hp.view("ident"), ins=["p512", "ident"], outs=["PB0"])
            fw.emit("vector", "tensor_copy", krT[:, 2048 * j + 128 * t:2048 * j + 128 * (t + 1)],
                    ctx.psum[0][:, 0:128], ins=["PB0"], outs=["krT"])
    hp.free("ABk")
    hp.live.pop("Ak", None)
    hp.live.pop("Bk", None)
    load_third(2)
    for t in range(NTILE):
        qkv_tile(t)
        fw.emit("vector", "tensor_copy", vbuf[:, 512 * t:512 * (t + 1)], ctx.psum[2][:],
                ins=["PB2"], outs=["vbuf"])
    load_third(0)
    for t in range(NTILE):
        qkv_tile(t)
        _qk_tile(ctx, ctx.psum[2][:], "PB2", "Aq", "Bq", t, hp.view("p512"), "p512", "sst")
        for j in range(4):
            fw.emit("tensor", "transpose", ctx.psum[0][:, 0:128], hp.view("p512")[:, 128 * j:128 * (j + 1)],
                    hp.view("ident"), ins=["p512", "ident"], outs=["PB0"])
            fw.emit("vector", "tensor_copy", qrT[:, 2048 * j + 128 * t:2048 * j + 128 * (t + 1)],
                    ctx.psum[0][:, 0:128], ins=["PB0"], outs=["qrT"])
    hp.free("xTs")
    hp.free("wthird")
    hp.free("ABq")
    for k in ("Aq", "Bq"):
        hp.live.pop(k, None)

    oT = hp.alloc("oT", 8192)
    fat = hp.alloc("fat", 640)
    at2 = hp.alloc("at2", 256, subs=("at2_0", "at2_1"))
    masks = ctx.masks
    for t in range(NTILE):
        kLo = max(0, 128 * t - 512)
        width = 128 * (t + 1) - kLo
        if t >= 4:
            msl = masks[:, 0:640]
        else:
            off = 640 + sum(128 * (j + 1) for j in range(t))
            msl = masks[:, off:off + width]
        otok = hp.view("p512")
        for h in range(8):
            j, half = h // 2, h % 2
            lq = qrT[64 * half:64 * half + 64, 2048 * j + 128 * t:2048 * j + 128 * (t + 1)]
            pdeps = []
            for s0 in range(0, width, 512):
                n = min(512, width - s0)
                lk = krT[64 * half:64 * half + 64, 2048 * j + kLo + s0:2048 * j + kLo + s0 + n]
                bk = f"PBIG{s0 // 512}"
                fw.emit("tensor", "matmul", ctx.pbig[:, s0:s0 + n], lq, lk,
                        start=True, stop=True, ins=["qrT", "krT"], outs=[bk])
                pdeps.append(bk)
            fw.emit("vector", "copy_predicated", ctx.pbig[:, 0:width], msl, hp.view("negbig")[:, 0:width],
                    ins=["masks", "negbig"] + pdeps, outs=pdeps)
            fw.emit("vector", "tensor_reduce", smf[:, 0:1], ctx.pbig[:, 0:width], axis=AX.X,
                    op=ALU.max, negate=True, ins=pdeps, outs=["smf0"])
            fw.emit("vector", "tensor_scalar", smf[:, 1:2], smf[:, 0:1], 8.0, None, op0=ALU.mult,
                    ins=["smf0"], outs=["smf1"])
            fw.emit("scalar", "activation", fat[:, 0:width], ctx.pbig[:, 0:width], AF.Exp,
                    bias=smf[:, 1:2], scale=8.0, accum_out=smf[:, 2:3],
                    ins=pdeps + ["smf1"], outs=["fat", "smf2"])
            fw.emit("vector", "reciprocal", smf[:, 3:4], smf[:, 2:3], ins=["smf2"], outs=["smf3"])
            fw.emit("vector", "tensor_scalar", fat[:, 0:width], fat[:, 0:width], smf[:, 3:4], None,
                    op0=ALU.mult, ins=["fat", "smf3"], outs=["fat"])
            first = True
            nkb = width // 128
            for kb in range(nkb):
                aslot = at2[:, 128 * (kb % 2):128 * (kb % 2 + 1)]
                akey = f"at2_{kb % 2}"
                fw.emit("tensor", "transpose", ctx.psum[0][:, 0:128], fat[:, 128 * kb:128 * (kb + 1)],
                        hp.view("ident"), ins=["fat", "ident"], outs=["PB0"])
                fw.emit("vector", "tensor_copy", aslot, ctx.psum[0][:, 0:128], ins=["PB0"], outs=[akey])
                tv = kLo // 128 + kb
                fw.emit("tensor", "matmul", ctx.psum[2][:, 0:64], aslot,
                        vbuf[:, 512 * tv + 64 * h:512 * tv + 64 * (h + 1)],
                        start=first, stop=(kb == nkb - 1), ins=[akey, "vbuf"],
                        outs=(["PB2"] if first else []), extra=([] if first else [fw.wr.get("PB2")]))
                first = False
            fw.wr["PB2"] = ("tensor", fw.cnt["tensor"])
            fw.emit("vector", "tensor_copy", otok[:, 64 * h:64 * (h + 1)], ctx.psum[2][:, 0:64],
                    ins=["PB2"], outs=["p512"])
        for j in range(4):
            fw.emit("tensor", "transpose", ctx.psum[0][:, 0:128], otok[:, 128 * j:128 * (j + 1)],
                    hp.view("ident"), ins=["p512", "ident"], outs=["PB0"])
            fw.emit("vector", "tensor_copy", oT[:, 2048 * j + 128 * t:2048 * j + 128 * (t + 1)],
                    ctx.psum[0][:, 0:128], ins=["PB0"], outs=["oT"])
    ctx.dump_raw(f"krT{l}", krT, ["krT"])
    ctx.dump_raw(f"qrT{l}", qrT, ["qrT"])
    ctx.dump_raw(f"v{l}", vbuf, ["vbuf"])
    ctx.dump_raw(f"oT{l}", oT, ["oT"])
    hp.free("qrT")
    hp.free("krT")
    hp.free("vbuf")

    wo = hp.alloc("wol", 4096)
    g = fw.group()
    fw.dma("sync", g, out=wo, in_=ctx.din("f_wo", [2, 512, D])[l].rearrange("(c p) f -> p c f", p=128), outs=["wol"])
    xnew = hp.alloc("x", BIGC)
    xb = hp.alloc("xb2", 2048, subs=("xb2_0", "xb2_1"))
    d = ctx.dram["spill_tmp"]
    gx = fw.group()
    for t in range(NTILE):
        xs = xb[:, 1024 * (t % 2):1024 * (t % 2 + 1)]
        xkey = f"xb2_{t % 2}"
        fw.dma("sync", gx, out=xs, in_=d[128 * t:128 * (t + 1), :], ins=[f"spill_tmp_t{t}"], outs=[xkey])
        for half in range(2):
            fw.mm_group(f"PBIG{half}",
                        [(ctx.pbig[:, 512 * half:512 * (half + 1)],
                          oT[:, 2048 * c + 128 * t:2048 * c + 128 * (t + 1)],
                          wo[:, 1024 * c + 512 * half:1024 * c + 512 * (half + 1)]) for c in range(4)],
                        ins=["oT", "wol"])
            fw.emit("vector", "tensor_tensor", xnew[:, 1024 * t + 512 * half:1024 * t + 512 * (half + 1)],
                    ctx.pbig[:, 512 * half:512 * (half + 1)], xs[:, 512 * half:512 * (half + 1)],
                    op=ALU.add, ins=[f"PBIG{half}", xkey], outs=["x"])
    hp.free("oT")
    hp.free("wol")
    hp.free("xb2")

    gcol = hp.alloc("gcol", 16)
    gg = fw.group()
    fw.dma("sync", gg, out=gcol[:, 0:8], in_=ctx.din("f_lng", [2, 128, 8])[l], outs=["gcol"])
    fw.dma("sync", gg, out=gcol[:, 8:16], in_=ctx.din("f_lnb", [2, 128, 8])[l], outs=["gcol"])
    stt = hp.alloc("stt", 168, subs=("stt_s1", "stt_s2", "stt_q1", "stt_q2",
                                     "stt_s2s", "stt_q2s", "stt_mu", "stt_rs",
                                     "stt_mun", "stt_rsn", "stt_t", "v0"))
    tok0 = fw.emit("vector", "memset", stt, 0.0, outs=["stt"])
    for sub in ("stt_s1", "stt_s2", "stt_q1", "stt_q2", "stt_s2s", "stt_q2s",
                "stt_mu", "stt_rs", "stt_mun", "stt_rsn", "stt_t", "v0"):
        fw.wr[sub] = tok0
    xv = hp.view("x")
    for t in range(NTILE):
        fw.emit("vector", "reduce_sum", stt[:, t:t + 1], xv[:, 1024 * t:1024 * t + 512], axis=AX.X,
                ins=["x"], outs=["stt_s1"])
        fw.emit("vector", "reduce_sum", stt[:, 16 + t:17 + t], xv[:, 1024 * t + 512:1024 * (t + 1)], axis=AX.X,
                ins=["x"], outs=["stt_s2"])
        fw.emit("scalar", "activation", ctx.psum[3][:], xv[:, 1024 * t:1024 * t + 512], AF.Square,
                accum_out=stt[:, 32 + t:33 + t], ins=["x"], outs=["PB3", "stt_q1"])
        fw.emit("scalar", "activation", ctx.psum[3][:], xv[:, 1024 * t + 512:1024 * (t + 1)], AF.Square,
                accum_out=stt[:, 48 + t:49 + t], ins=["x"], outs=["PB3", "stt_q2"])
    gs = fw.group("gpsimd")
    fw.dma("gpsimd", gs, out=stt[1:128, 64:80], in_=stt[0:127, 16:32], ins=["stt_s2"], outs=["stt_s2s"])
    fw.dma("gpsimd", gs, out=stt[0:1, 65:80], in_=stt[127:128, 16:31], ins=["stt_s2"], outs=["stt_s2s"])
    fw.dma("gpsimd", gs, out=stt[1:128, 80:96], in_=stt[0:127, 32 + 16:48 + 16], ins=["stt_q2"], outs=["stt_q2s"])
    fw.dma("gpsimd", gs, out=stt[0:1, 81:96], in_=stt[127:128, 48:63], ins=["stt_q2"], outs=["stt_q2s"])
    inv = 1.0 / 1024.0
    mu, rs = stt[:, 96:112], stt[:, 112:128]
    fw.emit("vector", "tensor_tensor", mu, stt[:, 0:16], stt[:, 64:80], op=ALU.add,
            ins=["stt_s1", "stt_s2s"], outs=["stt_mu"])
    fw.emit("vector", "tensor_scalar", mu, mu, inv, None, op0=ALU.mult, ins=["stt_mu"], outs=["stt_mu"])
    fw.emit("vector", "tensor_tensor", rs, stt[:, 32:48], stt[:, 80:96], op=ALU.add,
            ins=["stt_q1", "stt_q2s"], outs=["stt_rs"])
    fw.emit("vector", "tensor_scalar", rs, rs, inv, None, op0=ALU.mult, ins=["stt_rs"], outs=["stt_rs"])
    fw.emit("vector", "tensor_tensor", stt[:, 128:144], mu, mu, op=ALU.mult, ins=["stt_mu"], outs=["stt_t"])
    fw.emit("vector", "tensor_tensor", rs, rs, stt[:, 128:144], op=ALU.subtract,
            ins=["stt_rs", "stt_t"], outs=["stt_rs"])
    fw.emit("scalar", "activation", rs, rs, AF.Sqrt, bias=hp.view("epscol")[:, 0:1], scale=1.0,
            ins=["stt_rs", "epscol"], outs=["stt_rs"])
    fw.emit("vector", "reciprocal", rs, rs, ins=["stt_rs"], outs=["stt_rs"])
    gs2 = fw.group("gpsimd")
    fw.alias("stt_mun", ["stt_t"])
    fw.dma("gpsimd", gs2, out=stt[0:127, 128:144], in_=mu[1:128, :], ins=["stt_mu"], outs=["stt_mun"])
    fw.dma("gpsimd", gs2, out=stt[127:128, 128:143], in_=mu[0:1, 1:16], ins=["stt_mu"], outs=["stt_mun"])
    fw.dma("gpsimd", gs2, out=stt[0:127, 144:160], in_=rs[1:128, :], ins=["stt_rs"], outs=["stt_rsn"])
    fw.dma("gpsimd", gs2, out=stt[127:128, 144:159], in_=rs[0:1, 1:16], ins=["stt_rs"], outs=["stt_rsn"])
    mun, rsn = stt[:, 128:144], stt[:, 144:160]

    CW = 2056
    hnTs = hp.alloc("hnTs", 8 * CW)
    zb = hp.view("w512")
    for t in range(NTILE):
        fw.emit("vector", "tensor_scalar", zb, xv[:, 1024 * t:1024 * t + 512], mu[:, t:t + 1],
                rs[:, t:t + 1], op0=ALU.subtract, op1=ALU.mult,
                ins=["x", "stt_mu", "stt_rs"], outs=["w512"])
        for c in range(4):
            fw.emit("tensor", "transpose", ctx.psum[0][:, 0:128], zb[:, 128 * c:128 * (c + 1)],
                    hp.view("ident"), ins=["w512", "ident"], outs=["PB0"])
            fw.emit("vector", "tensor_scalar", hnTs[:, CW * c + 128 * t:CW * c + 128 * (t + 1)],
                    ctx.psum[0][:, 0:128], gcol[:, c:c + 1], gcol[:, 8 + c:9 + c],
                    op0=ALU.mult, op1=ALU.add, ins=["PB0", "gcol"], outs=["hnTs"])
        fw.emit("vector", "tensor_scalar", zb, xv[:, 1024 * t + 512:1024 * (t + 1)], mun[:, t:t + 1],
                rsn[:, t:t + 1], op0=ALU.subtract, op1=ALU.mult,
                ins=["x", "stt_mun", "stt_rsn"], outs=["w512"])
        for c in range(4):
            fw.emit("tensor", "transpose", ctx.psum[0][:, 0:128], zb[:, 128 * c:128 * (c + 1)],
                    hp.view("ident"), ins=["w512", "ident"], outs=["PB0"])
            fw.emit("vector", "tensor_scalar", hnTs[:, CW * (c + 4) + 1 + 128 * t:CW * (c + 4) + 1 + 128 * (t + 1)],
                    ctx.psum[0][:, 0:128], gcol[:, 4 + c:5 + c], gcol[:, 12 + c:13 + c],
                    op0=ALU.mult, op1=ALU.add, ins=["PB0", "gcol"], outs=["hnTs"])
    fw.emit("vector", "tensor_tensor", stt[0:1, 160:161], mu[0:1, 0:1], rs[0:1, 0:1], op=ALU.mult,
            ins=["stt_mu", "stt_rs"], outs=["v0"])
    fw.emit("vector", "tensor_scalar", stt[0:1, 160:161], stt[0:1, 160:161], -1.0, None, op0=ALU.mult,
            ins=["v0"], outs=["v0"])
    fw.emit("tensor", "matmul", ctx.psum[0][:, 0:1], hp.view("ones1")[0:1, :], stt[0:1, 160:161],
            start=True, stop=True, ins=["ones1", "v0"], outs=["PB0"])
    for c in range(4):
        fw.emit("vector", "tensor_scalar", hnTs[:, CW * (c + 4):CW * (c + 4) + 1],
                ctx.psum[0][:, 0:1], gcol[:, 4 + c:5 + c], gcol[:, 12 + c:13 + c],
                op0=ALU.mult, op1=ALU.add, ins=["PB0", "gcol"], outs=["hnTs"])

    for k in ("p512", "fat", "at2", "sst", "smf"):
        hp.free(k)
    hp.live.pop("at2_0", None)
    b1c = hp.alloc("b1c", 44)
    gb = fw.group()
    fw.dma("sync", gb, out=b1c, in_=ctx.din("f_b1c", [2, 128, 44])[l], outs=["b1c"])
    b2bc = hp.alloc("b2bc", 1024)
    b2r = hp.alloc("b2r", 1024)
    fw.dma("sync", gb, out=b2r[0:1, :], in_=ctx.din("f_b2", [2, D])[l:l + 1, :], outs=["b2r"])
    for half in range(2):
        fw.emit("tensor", "matmul", ctx.psum[0][:], hp.view("ones1")[0:1, :], b2r[0:1, 512 * half:512 * (half + 1)],
                start=True, stop=True, ins=["ones1", "b2r"], outs=["PB0"])
        fw.emit("vector", "tensor_copy", b2bc[:, 512 * half:512 * (half + 1)], ctx.psum[0][:],
                ins=["PB0"], outs=["b2bc"])
    hp.free("b2r")
    pbufA = hp.alloc("pbufA", 16 * 512, subs=tuple(f"pbuf{i}" for i in range(16)))
    pbufB = hp.alloc("pbufB", 6 * 512, subs=tuple(f"pbuf{i}" for i in range(16, 22)))
    pbuf_sl = lambda i: (pbufA[:, 512 * i:512 * (i + 1)] if i < 16
                         else pbufB[:, 512 * (i - 16):512 * (i - 15)])
    w1s = hp.alloc("w1s", 2048)
    w2s = hp.alloc("w2s", 1024)
    abuf = hp.view("w512")
    gbuf = hp.view("t512")
    w1d = ctx.din("f_w1p", [2, 22, 128, 8, 2, 128])
    w2d = ctx.din("f_w2p", [2, 22, 128, 1024])
    for blk in range(4):
        for i in range(22):
            wsl = w1s
            wkey = "w1s"
            gw = fw.group()
            fw.dma("sync", gw, out=wsl, in_=w1d[l, i].rearrange("p c h f -> p (c h f)"), outs=[wkey])
            fw.mm_group("PBIG0", [(ctx.pbig[:, 0:512], wsl[:, 256 * c:256 * c + 128],
                                   hnTs[:, CW * c + 512 * blk:CW * c + 512 * (blk + 1)]) for c in range(NCHUNK)],
                        ins=[wkey, "hnTs"])
            fw.mm_group("PBIG1", [(ctx.pbig[:, 512:1024], wsl[:, 256 * c + 128:256 * c + 256],
                                   hnTs[:, CW * c + 512 * blk:CW * c + 512 * (blk + 1)]) for c in range(NCHUNK)],
                        ins=[wkey, "hnTs"])
            fw.emit("vector", "tensor_scalar", abuf, ctx.pbig[:, 0:512], b1c[:, i:i + 1], None,
                    op0=ALU.add, ins=["PBIG0", "b1c"], outs=["w512"])
            fw.emit("scalar", "activation", gbuf, ctx.pbig[:, 512:1024], AF.Gelu,
                    bias=b1c[:, 22 + i:23 + i], scale=1.0, ins=["PBIG1", "b1c"], outs=["t512"])
            fw.emit("vector", "tensor_tensor", pbuf_sl(i), abuf, gbuf, op=ALU.mult,
                    ins=["w512", "t512"], outs=[f"pbuf{i}"])
        for i in range(22):
            wsl = w2s
            wkey = "w2s"
            gw = fw.group()
            fw.dma("sync", gw, out=wsl, in_=w2d[l, i], outs=[wkey])
            for t4 in range(4):
                for half in range(2):
                    if half == 0:
                        po, pk = ctx.psum[t4][:], f"PB{t4}"
                    else:
                        po, pk = ctx.pbig[:, 512 * t4:512 * (t4 + 1)], f"PBIG{t4}"
                    fw.emit("tensor", "matmul", po,
                            pbuf_sl(i)[:, 128 * t4:128 * (t4 + 1)],
                            wsl[:, 512 * half:512 * (half + 1)],
                            start=(i == 0), stop=(i == 21),
                            ins=[f"pbuf{i}", wkey],
                            outs=([pk] if i == 0 else []),
                            extra=([] if i == 0 else [fw.wr.get(pk)]))
                    if i == 21:
                        fw.wr[pk] = ("tensor", fw.cnt["tensor"])
        for t4 in range(4):
            t = 4 * blk + t4
            for half in range(2):
                if half == 0:
                    po, pk = ctx.psum[t4][:], f"PB{t4}"
                else:
                    po, pk = ctx.pbig[:, 512 * t4:512 * (t4 + 1)], f"PBIG{t4}"
                sl = slice(1024 * t + 512 * half, 1024 * t + 512 * (half + 1))
                fw.emit("vector", "tensor_tensor", gbuf, po, b2bc[:, 512 * half:512 * (half + 1)],
                        op=ALU.add, ins=[pk, "b2bc"], outs=["t512"])
                fw.emit("vector", "tensor_tensor", xv[:, sl], gbuf, xv[:, sl], op=ALU.add,
                        ins=["t512", "x"], outs=["x"])
    for k in ("hnTs", "pbufA", "pbufB", "w1s", "w2s", "b1c", "b2bc", "gcol", "stt",
              "w512", "t512"):
        hp.free(k)


def build_l1_attn(ctx):
    nc, fw, hp = ctx.nc, ctx.fw, ctx.heap
    spill_residual(ctx, "spill_l1a")
    hp.free("x")

    y = hp.alloc("y", BIGC)
    kT = hp.alloc("kT", 6144, subs=tuple(f"kTs{s}" for s in range(3)))
    vr = hp.alloc("vr", 6144, subs=tuple(f"vr{s}" for s in range(6)))
    qT = hp.alloc("qT", 4096, subs=("qT0", "qT1"))
    xTb = hp.alloc("xTb", 2048)
    xrb = hp.alloc("xrb", 4096, subs=tuple(f"xrbt{j}" for j in range(4)))
    wqk = hp.alloc("wqk", 4096, subs=("wqk0", "wqk1"))
    wvs = hp.alloc("wvs", 2048, subs=("wvs0", "wvs1"))
    bqk = hp.alloc("bqk", 16)
    bvbc = hp.alloc("bvbc", 1024)
    afat = hp.alloc("afat", 512)
    at2 = hp.alloc("at2", 256, subs=("at2_0", "at2_1"))
    sm = hp.alloc("smL", 8, subs=("smL0", "smL1", "smL2", "smL3"))

    g0 = fw.group()
    fw.dma("sync", g0, out=bqk[:, 0:8], in_=ctx.din("g_bqk", [128, 16])[:, 0:8], outs=["bqk"])
    fw.dma("sync", g0, out=bqk[:, 8:16], in_=ctx.din("g_bqk", [128, 16])[:, 8:16], outs=["bqk"])
    bvr = hp.alloc("bvr", 1024)
    fw.dma("sync", g0, out=bvr[0:1, :], in_=ctx.din("g_bv", [1, D]), outs=["bvr"])
    for half in range(2):
        fw.emit("tensor", "matmul", ctx.psum[0][:], hp.view("ones1")[0:1, :],
                bvr[0:1, 512 * half:512 * (half + 1)], start=True, stop=True,
                ins=["ones1", "bvr"], outs=["PB0"])
        fw.emit("vector", "tensor_copy", bvbc[:, 512 * half:512 * (half + 1)], ctx.psum[0][:],
                ins=["PB0"], outs=["bvbc"])
    hp.free("bvr")

    d = ctx.dram["spill_l1a"]
    qkd = ctx.din("g_wqk", [8, 128, 2048])
    wvd = ctx.din("g_wv", [8, 128, 1024])
    NB = 8

    def projections(b):
        gr = fw.group()
        tkeys = []
        for j in range(2):
            t = 2 * b + j
            sl = xrb[:, 1024 * (t % 4):1024 * (t % 4 + 1)]
            tk = f"xrbt{t % 4}"
            fw.dma("sync", gr, out=sl, in_=d[128 * t:128 * (t + 1), :],
                   ins=[f"spill_l1a_t{t}"], outs=[tk])
            tkeys.append(tk)
            for c in range(NCHUNK):
                pb = ctx.psum[c % 2]
                pk = f"PB{c % 2}"
                fw.emit("tensor", "transpose", pb[:, 0:128], sl[:, 128 * c:128 * (c + 1)],
                        hp.view("ident"), ins=[tk, "ident"], outs=[pk])
                fw.emit("vector", "tensor_copy", xTb[:, 256 * c + 128 * j:256 * c + 128 * (j + 1)],
                        pb[:, 0:128], ins=[pk], outs=["xTb"])
        qsl = 2048 * (b % 2)
        qkey = f"qT{b % 2}"
        for c in range(NCHUNK):
            wsl = wqk[:, 2048 * (c % 2):2048 * (c % 2 + 1)]
            wk_ = f"wqk{c % 2}"
            gw = fw.group()
            fw.dma("sync", gw, out=wsl, in_=qkd[c], outs=[wk_])
            for gsl in range(NCHUNK):
                st = (c == 0 and gsl % 2 == 0)
                sp = (c == NCHUNK - 1 and gsl % 2 == 1)
                fw.emit("tensor", "matmul", ctx.pbig[:, 256 * gsl:256 * (gsl + 1)],
                        wsl[:, 128 * gsl:128 * (gsl + 1)], xTb[:, 256 * c:256 * (c + 1)],
                        start=st, stop=sp,
                        ins=[wk_, "xTb"], outs=([f"PBIG{gsl // 2}"] if st else []),
                        extra=([] if st else [fw.wr.get(f"PBIG{gsl // 2}")]))
                if sp:
                    fw.wr[f"PBIG{gsl // 2}"] = ("tensor", fw.cnt["tensor"])
                fw.emit("tensor", "matmul", ctx.psum[gsl // 2][:, 256 * (gsl % 2):256 * (gsl % 2 + 1)],
                        wsl[:, 1024 + 128 * gsl:1024 + 128 * (gsl + 1)], xTb[:, 256 * c:256 * (c + 1)],
                        start=st, stop=sp,
                        ins=[wk_, "xTb"], outs=([f"PB{gsl // 2}"] if st else []),
                        extra=([] if st else [fw.wr.get(f"PB{gsl // 2}")]))
                if sp:
                    fw.wr[f"PB{gsl // 2}"] = ("tensor", fw.cnt["tensor"])
        s = b % 3
        for gsl in range(NCHUNK):
            fw.emit("vector", "tensor_scalar", qT[:, qsl + 256 * gsl:qsl + 256 * (gsl + 1)],
                    ctx.pbig[:, 256 * gsl:256 * (gsl + 1)], bqk[:, gsl:gsl + 1], None,
                    op0=ALU.add, ins=[f"PBIG{gsl // 2}", "bqk"], outs=[qkey])
            fw.emit("vector", "tensor_scalar", kT[:, 768 * gsl + 256 * s:768 * gsl + 256 * (s + 1)],
                    ctx.psum[gsl // 2][:, 256 * (gsl % 2):256 * (gsl % 2 + 1)],
                    bqk[:, 8 + gsl:9 + gsl], None,
                    op0=ALU.add, ins=[f"PB{gsl // 2}", "bqk"], outs=[f"kTs{s}"])
        for c in range(NCHUNK):
            wsl = wvs[:, 1024 * (c % 2):1024 * (c % 2 + 1)]
            wk_ = f"wvs{c % 2}"
            gw = fw.group()
            fw.dma("sync", gw, out=wsl, in_=wvd[c], outs=[wk_])
            for j in range(2):
                for half in range(2):
                    pk = f"PBIG{2 + j}" if half == 0 else f"PB{2 + j}"
                    po = ctx.pbig[:, 1024 + 512 * j:1536 + 512 * j] if half == 0 else ctx.psum[2 + j][:]
                    fw.emit("tensor", "matmul", po,
                            xTb[:, 256 * c + 128 * j:256 * c + 128 * (j + 1)],
                            wsl[:, 512 * half:512 * (half + 1)],
                            start=(c == 0), stop=(c == NCHUNK - 1),
                            ins=[wk_, "xTb"], outs=([pk] if c == 0 else []),
                            extra=([] if c == 0 else [fw.wr.get(pk)]))
                    if c == NCHUNK - 1:
                        fw.wr[pk] = ("tensor", fw.cnt["tensor"])
        for j in range(2):
            t = 2 * b + j
            rs = t % 6
            for half in range(2):
                pk = f"PBIG{2 + j}" if half == 0 else f"PB{2 + j}"
                po = ctx.pbig[:, 1024 + 512 * j:1536 + 512 * j] if half == 0 else ctx.psum[2 + j][:]
                fw.emit("vector", "tensor_tensor", vr[:, 1024 * rs + 512 * half:1024 * rs + 512 * (half + 1)],
                        po, bvbc[:, 512 * half:512 * (half + 1)], op=ALU.add,
                        ins=[pk, "bvbc"], outs=[f"vr{rs}"])

    def attention_now(b):
        kdep = [f"kTs{bb % 3}" for bb in range(max(0, b - 1), min(NB, b + 2))]
        qsl = 2048 * (b % 2)
        qkey = f"qT{b % 2}"
        for jq in range(2):
            t = 2 * b + jq
            pieces = []
            if b > 0:
                pieces.append((0, (b - 1) % 3, 128, 128))
            pieces.append((128, b % 3, 0, 256))
            if b < NB - 1:
                pieces.append((384, (b + 1) % 3, 0, 128))
            items = []
            for (pc, s, so, w) in pieces:
                for c in range(NCHUNK):
                    items.append((ctx.pbig[:, pc:pc + w],
                                  qT[:, qsl + 256 * c + 128 * jq:qsl + 256 * c + 128 * (jq + 1)],
                                  kT[:, 768 * c + 256 * s + so:768 * c + 256 * s + so + w]))
            fw.mm_group("PBIG0", items, ins=[qkey] + kdep)
            if b == 0 and jq == 0:
                fw.emit("vector", "tensor_copy", ctx.pbig[:, 0:128], hp.view("negbig")[:, 0:128],
                        ins=["negbig"], outs=["PBIG0"], extra=[fw.wr.get("PBIG0")])
            if b == 0 and jq == 1:
                fw.emit("vector", "tensor_copy", ctx.pbig[:, 0:128], hp.view("negbig")[:, 0:128],
                        ins=["negbig"], outs=["PBIG0"], extra=[fw.wr.get("PBIG0")])
            if b == NB - 1:
                fw.emit("vector", "tensor_copy", ctx.pbig[:, 384:512], hp.view("negbig")[:, 0:128],
                        ins=["negbig"], outs=["PBIG0"], extra=[fw.wr.get("PBIG0")])
            moff = 1920 + 512 * jq
            fw.emit("vector", "copy_predicated", ctx.pbig[:, 0:512], ctx.masks[:, moff:moff + 512],
                    hp.view("negbig")[:, 0:512], ins=["masks", "negbig", "PBIG0"], outs=["PBIG0"])
            fw.emit("vector", "tensor_reduce", sm[:, 0:1], ctx.pbig[:, 0:512], axis=AX.X,
                    op=ALU.max, negate=True, ins=["PBIG0"], outs=["smL0"])
            fw.emit("vector", "tensor_scalar", sm[:, 1:2], sm[:, 0:1], 0.03125, None,
                    op0=ALU.mult, ins=["smL0"], outs=["smL1"])
            fw.emit("scalar", "activation", afat[:, 0:512], ctx.pbig[:, 0:512], AF.Exp,
                    bias=sm[:, 1:2], scale=0.03125, accum_out=sm[:, 2:3],
                    ins=["PBIG0", "smL1"], outs=["afat", "smL2"])
            fw.emit("vector", "reciprocal", sm[:, 3:4], sm[:, 2:3], ins=["smL2"], outs=["smL3"])
            fw.emit("vector", "tensor_scalar", afat[:, 0:512], afat[:, 0:512], sm[:, 3:4], None,
                    op0=ALU.mult, ins=["afat", "smL3"], outs=["afat"])
            first = True
            for kb in range(4):
                tv = 2 * b - 1 + kb
                if tv < 0 or tv >= 16:
                    continue
                aslot = at2[:, 128 * (kb % 2):128 * (kb % 2 + 1)]
                akey = f"at2_{kb % 2}"
                fw.emit("tensor", "transpose", ctx.psum[0][:, 0:128], afat[:, 128 * kb:128 * (kb + 1)],
                        hp.view("ident"), ins=["afat", "ident"], outs=["PB0"])
                fw.emit("vector", "tensor_copy", aslot, ctx.psum[0][:, 0:128], ins=["PB0"], outs=[akey])
                for half in range(2):
                    pk = f"PBIG{2 + half}"
                    fw.emit("tensor", "matmul", ctx.pbig[:, 1024 + 512 * half:1536 + 512 * half],
                            aslot, vr[:, 1024 * (tv % 6) + 512 * half:1024 * (tv % 6) + 512 * (half + 1)],
                            start=first, stop=(kb == 3 or (b == NB - 1 and kb == 2)),
                            ins=[akey, f"vr{tv % 6}"], outs=([pk] if first else []),
                            extra=([] if first else [fw.wr.get(pk)]))
                    if kb == 3 or (b == NB - 1 and kb == 2):
                        fw.wr[pk] = ("tensor", fw.cnt["tensor"])
                first = False
            for half in range(2):
                sl = slice(1024 * t + 512 * half, 1024 * t + 512 * (half + 1))
                fw.emit("vector", "tensor_tensor", y[:, sl],
                        ctx.pbig[:, 1024 + 512 * half:1536 + 512 * half],
                        xrb[:, 1024 * (t % 4) + 512 * half:1024 * (t % 4) + 512 * (half + 1)],
                        op=ALU.add, ins=[f"PBIG{2 + half}", f"xrbt{t % 4}"], outs=["y"])

    for b in range(NB):
        projections(b)
        if b >= 1:
            attention_now(b - 1)
    attention_now(NB - 1)

    for k in ("kT", "vr", "qT", "xTb", "xrb", "wqk", "wvs", "bqk", "bvbc", "afat", "at2", "smL"):
        hp.free(k)
    xn = hp.alloc("x", BIGC)
    postln(ctx, "y", "x")
    hp.free("y")
    ctx.maybe_dump("x", "x3")


def emit_round(ctx, v_ap, v_keys, out_ap, out_key, qs1, qs2, k1, k2):
    fw = ctx.fw
    fw.emit("vector", "tensor_scalar", qs1, v_ap, 128.5, None, op0=ALU.add,
            ins=list(v_keys), outs=[k1])
    fw.emit("vector", "tensor_copy", qs2, qs1, ins=[k1], outs=[k2])
    for p in (128, 64, 32, 16, 8, 4, 2, 1):
        fw.emit("vector", "tensor_scalar", out_ap, qs2, float(p), None, op0=ALU.is_ge,
                ins=[k2], outs=[out_key])
        fw.emit("vector", "scalar_tensor_tensor", qs2, out_ap, float(-p), qs2,
                op0=ALU.mult, op1=ALU.add, ins=[out_key, k2], outs=[k2])
    fw.emit("vector", "scalar_tensor_tensor", out_ap, qs2, -1.0, qs1,
            op0=ALU.mult, op1=ALU.add, ins=[k1, k2], outs=[out_key])
    fw.emit("vector", "tensor_scalar", out_ap, out_ap, -128.0, None, op0=ALU.add,
            ins=[out_key], outs=[out_key])


def build_l1_ffn(ctx):
    nc, fw, hp = ctx.nc, ctx.fw, ctx.heap
    spill_residual(ctx, "spill_l1f")
    CH, NI = 4096, 32

    st = hp.alloc("qst", 64, subs=("qst_m", "qst_s", "qst_r", "qst_re"))
    q1 = hp.alloc("qs1", 1024)
    q2 = hp.alloc("qs2", 1024)
    xv = hp.view("x")
    for t in range(NTILE):
        fw.emit("vector", "tensor_reduce", st[:, t:t + 1], xv[:, 1024 * t:1024 * (t + 1)],
                axis=AX.X, op=ALU.max, apply_absolute_value=True, ins=["x"], outs=["qst_m"])
    fw.emit("vector", "tensor_scalar", st[:, 0:16], st[:, 0:16], 1e-5, None, op0=ALU.max,
            ins=["qst_m"], outs=["qst_m"])
    fw.emit("vector", "reciprocal", st[:, 16:32], st[:, 0:16], ins=["qst_m"], outs=["qst_s"])
    fw.emit("vector", "tensor_scalar", st[:, 16:32], st[:, 16:32], 127.0, None, op0=ALU.mult,
            ins=["qst_s"], outs=["qst_s"])
    fw.emit("vector", "tensor_scalar", st[:, 32:48], st[:, 0:16], 1.0 / 127.0, None, op0=ALU.mult,
            ins=["qst_m"], outs=["qst_r"])
    for t in range(NTILE):
        sl = xv[:, 1024 * t:1024 * (t + 1)]
        fw.emit("vector", "tensor_scalar", sl, sl, st[:, 16 + t:17 + t], None, op0=ALU.mult,
                ins=["x", "qst_s"], outs=["x"])
        emit_round(ctx, sl, ["x"], sl, "x", q1, q2, "rq1", "rq2")
    ixT = hp.alloc("ixT", BIGC)
    for c in range(NCHUNK):
        for t in range(NTILE):
            pb = ctx.psum[t % 2]
            pk = f"PB{t % 2}"
            fw.emit("tensor", "transpose", pb[:, 0:128], xv[:, 1024 * t + 128 * c:1024 * t + 128 * (c + 1)],
                    hp.view("ident"), ins=["x", "ident"], outs=[pk])
            fw.emit("vector", "tensor_copy", ixT[:, 2048 * c + 128 * t:2048 * c + 128 * (t + 1)],
                    pb[:, 0:128], ins=[pk], outs=["ixT"])
    hp.free("x")
    hp.free("qs1")
    hp.free("qs2")

    wq1s = nc.dram_tensor("wq1s", [D, CH], F32).ap()
    wq2s = nc.dram_tensor("wq2s", [CH, D], F32).ap()
    w1d = ctx.din("h_w1T", [D, CH])
    w2d = ctx.din("h_w2T", [CH, D])
    wt = hp.alloc("wt", 4096, subs=("wt0",))
    wb1 = hp.alloc("wb1", 4096)
    wb2 = hp.alloc("wb2", 4096)
    wsc = hp.alloc("wsc", 16, subs=("acc", "onescol", "wtot", "thrs", "wred", "mw", "mwj"))
    fw.emit("gpsimd", "memset", wsc[:, 1:2], 1.0, outs=["onescol"])
    for wi, (wd, wqs, nch, nel) in enumerate([(w1d, wq1s, NCHUNK, D * CH), (w2d, wq2s, NI, CH * D)]):
        fw.emit("gpsimd", "memset", wsc[:, 0:1], 0.0, outs=["acc"])
        width = CH if wi == 0 else D
        for c in range(nch):
            g = fw.group()
            fw.dma("sync", g, out=wt[:, 0:width], in_=wd[128 * c:128 * (c + 1), :], outs=["wt0"])
            fw.emit("vector", "tensor_reduce", wsc[:, 8:9], wt[:, 0:width], axis=AX.X, op=ALU.add,
                    apply_absolute_value=True, ins=["wt0"], outs=["wred"])
            fw.emit("vector", "tensor_tensor", wsc[:, 0:1], wsc[:, 0:1], wsc[:, 8:9], op=ALU.add,
                    ins=["acc", "wred"], outs=["acc"])
        fw.emit("tensor", "matmul", ctx.psum[1][0:1, 0:1], wsc[:, 0:1], wsc[:, 1:2],
                start=True, stop=True, ins=["acc", "onescol"], outs=["PB1"])
        fw.emit("vector", "tensor_scalar", wsc[0:1, 9:10], ctx.psum[1][0:1, 0:1], 1.0 / nel, 1e-5,
                op0=ALU.mult, op1=ALU.max, ins=["PB1"], outs=["mw"])
        base = 2 + 3 * wi
        for j, mulv in enumerate((0.5, -0.5, 1.0)):
            fw.emit("vector", "tensor_scalar", wsc[0:1, 10:11], wsc[0:1, 9:10], mulv, None,
                    op0=ALU.mult, ins=["mw"], outs=["mwj"])
            fw.emit("tensor", "matmul", ctx.psum[1][:, 4 + j:5 + j], hp.view("ones1")[0:1, :],
                    wsc[0:1, 10:11], start=True, stop=True, ins=["ones1", "mwj"], outs=["PB1"])
        fw.emit("vector", "tensor_copy", wsc[:, base:base + 3], ctx.psum[1][:, 4:7],
                ins=["PB1"], outs=["thrs"])
        for c in range(nch):
            g = fw.group()
            fw.dma("sync", g, out=wt[:, 0:width], in_=wd[128 * c:128 * (c + 1), :], outs=["wt0"])
            fw.emit("vector", "tensor_scalar", wb1[:, 0:width], wt[:, 0:width], wsc[:, base:base + 1],
                    None, op0=ALU.is_ge, ins=["wt0", "thrs"], outs=["wb1"])
            fw.emit("vector", "tensor_scalar", wb2[:, 0:width], wt[:, 0:width], wsc[:, base + 1:base + 2],
                    None, op0=ALU.is_le, ins=["wt0", "thrs"], outs=["wb2"])
            fw.emit("vector", "tensor_tensor", wb1[:, 0:width], wb1[:, 0:width], wb2[:, 0:width],
                    op=ALU.subtract, ins=["wb1", "wb2"], outs=["wb1"])
            g2 = fw.group("gpsimd")
            fw.dma("gpsimd", g2, out=wqs[128 * c:128 * (c + 1), :], in_=wb1[:, 0:width],
                   ins=["wb1"], outs=[f"wq{wi}c{c}"])
    hp.free("wt")
    hp.free("wb1")
    hp.free("wb2")

    ccol = hp.alloc("ccol", 96)
    bcol = hp.alloc("bcol", 32)
    g = fw.group()
    fw.dma("sync", g, out=ccol, in_=ctx.din("h_ccol", [128, 96]), outs=["ccol"])
    fw.dma("sync", g, out=bcol, in_=ctx.din("h_bcol", [128, 32]), outs=["bcol"])

    ybuf = hp.alloc("ybuf", NI * 512, subs=tuple(f"yb{i}" for i in range(NI)))
    wq1c = hp.alloc("wq1c", 1024, subs=("wq1c_",))
    wq2c = hp.alloc("wq2c", 1024, subs=("wq2c_",))
    hext = hp.alloc("hext", 516)
    cbuf = hp.alloc("cbuf", 512)
    m2b = hp.alloc("m2b", 512)
    rxbc = hp.alloc("rxbc", 512)
    sybc = hp.alloc("sybc", 512)
    lst = hp.alloc("lst", 64, subs=("lst_",))
    syt = hp.alloc("syt", 16, subs=("syt_m", "syt_u"))
    qq1 = hp.alloc("qq1", 512)
    qq2 = hp.alloc("qq2", 512)
    x3b = hp.alloc("x3b", 2048, subs=("x3b0", "x3b1"))
    ytl = hp.alloc("ytl", 1024)
    lnc = hp.alloc("lnc", 8, subs=("lnc_s", "lnc_q", "lnc_mu", "lnc_rs"))
    fw.emit("gpsimd", "memset", lst, 0.0, outs=["lst_"])
    x4piece = {}

    def x4_write(t, half, src_ap, src_keys):
        b, t4 = divmod(t, 4)
        c = 2 * t4 + half
        dst = hp.view("ixT")[:, 2048 * c + 512 * b:2048 * c + 512 * (b + 1)]
        key = f"x4_{t}_{half}"
        fw.alias(key, ["ixT"])
        fw.emit("vector", "tensor_copy", dst, src_ap, ins=list(src_keys), outs=[key])
        x4piece[(t, half)] = (dst, key)

    d3 = ctx.dram["spill_l1f"]
    for b in range(4):
        grx = fw.group()
        for t4 in range(4):
            col = bass.AP(st.tensor, st.offset + 32 + 4 * b + t4, [[st.ap[0][0], 128], [1, 1]])
            fw.dma("sync", grx, out=ytl[0:1, 128 * t4:128 * (t4 + 1)], in_=col,
                   ins=["qst_r"], outs=["ytl"], allow_slow=True)
        fw.emit("tensor", "matmul", ctx.psum[1][:], hp.view("ones1")[0:1, :], ytl[0:1, 0:512],
                start=True, stop=True, ins=["ones1", "ytl"], outs=["PB1"])
        fw.emit("vector", "tensor_scalar", rxbc, ctx.psum[1][:], wsc[:, 4:5], None, op0=ALU.mult,
                ins=["PB1", "thrs"], outs=["rxbc"])
        if b < 3:
            fw.emit("tensor", "matmul", ctx.psum[1][:, 0:1], hp.view("ones1")[0:1, :],
                    st[0:1, 36 + 4 * b:37 + 4 * b], start=True, stop=True,
                    ins=["ones1", "qst_r"], outs=["PB1"])
            fw.emit("vector", "tensor_scalar", st[:, 48:49], ctx.psum[1][:, 0:1], wsc[:, 4:5], None,
                    op0=ALU.mult, ins=["PB1", "thrs"], outs=["qst_re"])
        fw.emit("gpsimd", "memset", m2b, 0.0, outs=["m2b"])
        for i in range(NI):
            g = fw.group()
            fw.dma("sync", g, out=wq1c,
                   in_=wq1s[:, 128 * i:128 * (i + 1)].rearrange("(c p) f -> p c f", p=128),
                   ins=[f"wq0c{c}" for c in range(NCHUNK)], outs=["wq1c_"])
            fw.mm_group("PB0", [(ctx.psum[0][:], wq1c[:, 128 * c:128 * (c + 1)],
                                 hp.view("ixT")[:, 2048 * c + 512 * b:2048 * c + 512 * (b + 1)])
                                for c in range(NCHUNK)], ins=["wq1c_", "ixT"])
            fw.emit("vector", "tensor_copy", hext[:, 0:1], lst[:, i:i + 1], ins=["lst_"], outs=["hext"])
            fw.emit("vector", "tensor_tensor", hext[:, 1:513], ctx.psum[0][:], rxbc, op=ALU.mult,
                    ins=["PB0", "rxbc"], outs=["hext"])
            if b < 3:
                fw.mm_group("PB1", [(ctx.psum[1][:, 0:1], wq1c[:, 128 * c:128 * (c + 1)],
                                     hp.view("ixT")[:, 2048 * c + 512 * (b + 1):2048 * c + 512 * (b + 1) + 1])
                                    for c in range(NCHUNK)], ins=["wq1c_", "ixT"])
                fw.emit("vector", "tensor_tensor", hext[:, 513:514], ctx.psum[1][:, 0:1], st[:, 48:49],
                        op=ALU.mult, ins=["PB1", "qst_re"], outs=["hext"])
            else:
                fw.emit("gpsimd", "memset", hext[:, 513:514], 0.0, outs=["hext"])
            fw.emit("vector", "tensor_copy", lst[:, i:i + 1], hext[:, 512:513], ins=["hext"], outs=["lst_"])
            fw.emit("vector", "tensor_scalar", cbuf, hext[:, 0:512], ccol[:, i:i + 1], None,
                    op0=ALU.mult, ins=["hext", "ccol"], outs=["cbuf"])
            fw.emit("vector", "scalar_tensor_tensor", cbuf, hext[:, 1:513], ccol[:, 32 + i:33 + i],
                    cbuf, op0=ALU.mult, op1=ALU.add, ins=["hext", "ccol", "cbuf"], outs=["cbuf"])
            fw.emit("vector", "scalar_tensor_tensor", cbuf, hext[:, 2:514], ccol[:, 64 + i:65 + i],
                    cbuf, op0=ALU.mult, op1=ALU.add, ins=["hext", "ccol", "cbuf"], outs=["cbuf"])
            ysl = ybuf[:, 512 * i:512 * (i + 1)]
            fw.emit("scalar", "activation", ysl, cbuf, AF.Silu, bias=bcol[:, i:i + 1], scale=1.0,
                    ins=["cbuf", "bcol"], outs=[f"yb{i}"])
            fw.emit("vector", "tensor_tensor", m2b, m2b, ysl, op=ALU.max,
                    ins=["m2b", f"yb{i}"], outs=["m2b"])
            fw.emit("vector", "scalar_tensor_tensor", m2b, ysl, -1.0, m2b,
                    op0=ALU.mult, op1=ALU.max, ins=["m2b", f"yb{i}"], outs=["m2b"])
        for t4 in range(4):
            fw.emit("tensor", "transpose", ctx.psum[1][:, 0:128], m2b[:, 128 * t4:128 * (t4 + 1)],
                    hp.view("ident"), ins=["m2b", "ident"], outs=["PB1"])
            fw.emit("vector", "tensor_reduce", syt[:, t4:t4 + 1], ctx.psum[1][:, 0:128],
                    axis=AX.X, op=ALU.max, ins=["PB1"], outs=["syt_m"])
        fw.emit("vector", "tensor_scalar", syt[:, 0:4], syt[:, 0:4], 1e-5, None, op0=ALU.max,
                ins=["syt_m"], outs=["syt_m"])
        fw.emit("vector", "reciprocal", syt[:, 4:8], syt[:, 0:4], ins=["syt_m"], outs=["syt_u"])
        fw.emit("vector", "tensor_scalar", syt[:, 4:8], syt[:, 4:8], 127.0, None, op0=ALU.mult,
                ins=["syt_u"], outs=["syt_u"])
        fw.emit("vector", "tensor_scalar", syt[:, 8:12], syt[:, 0:4], 1.0 / 127.0, None, op0=ALU.mult,
                ins=["syt_m"], outs=["syt_m2"])
        fw.emit("vector", "tensor_scalar", syt[:, 12:16], syt[:, 8:12], wsc[:, 7:8], None, op0=ALU.mult,
                ins=["syt_m2", "thrs"], outs=["syt_us"])
        gsy = fw.group()
        for t4 in range(4):
            col = bass.AP(syt.tensor, syt.offset + 4 + t4, [[syt.ap[0][0], 128], [1, 1]])
            fw.dma("sync", gsy, out=ytl[0:1, 128 * t4:128 * (t4 + 1)], in_=col,
                   ins=["syt_u"], outs=["ytl"], allow_slow=True)
        fw.emit("tensor", "matmul", ctx.psum[1][:], hp.view("ones1")[0:1, :], ytl[0:1, 0:512],
                start=True, stop=True, ins=["ones1", "ytl"], outs=["PB1"])
        fw.emit("vector", "tensor_copy", sybc, ctx.psum[1][:], ins=["PB1"], outs=["sybc"])
        for i in range(NI):
            ysl = ybuf[:, 512 * i:512 * (i + 1)]
            fw.emit("vector", "tensor_tensor", ysl, ysl, sybc, op=ALU.mult,
                    ins=[f"yb{i}", "sybc"], outs=[f"yb{i}"])
            emit_round(ctx, ysl, [f"yb{i}"], ysl, f"yb{i}", qq1, qq2, "rqa", "rqb")
        for i in range(NI):
            g = fw.group()
            fw.dma("sync", g, out=wq2c, in_=wq2s[128 * i:128 * (i + 1), :],
                   ins=[f"wq1c{i}"], outs=["wq2c_"])
            for t4 in range(4):
                for half in range(2):
                    if half == 0:
                        po, pk = ctx.psum[t4][:], f"PB{t4}"
                    else:
                        po, pk = ctx.pbig[:, 512 * t4:512 * (t4 + 1)], f"PBIG{t4}"
                    fw.emit("tensor", "matmul", po, ybuf[:, 512 * i + 128 * t4:512 * i + 128 * (t4 + 1)],
                            wq2c[:, 512 * half:512 * (half + 1)],
                            start=(i == 0), stop=(i == NI - 1),
                            ins=[f"yb{i}", "wq2c_"],
                            outs=([pk] if i == 0 else []),
                            extra=([] if i == 0 else [fw.wr.get(pk)]))
                    if i == NI - 1:
                        fw.wr[pk] = ("tensor", fw.cnt["tensor"])
        for t4 in range(4):
            t = 4 * b + t4
            xsl = x3b[:, 1024 * (t % 2):1024 * (t % 2 + 1)]
            xkey = f"x3b{t % 2}"
            fw.dma("sync", fw.group(), out=xsl, in_=d3[128 * t:128 * (t + 1), :],
                   ins=[f"spill_l1f_t{t}"], outs=[xkey])
            for half in range(2):
                if half == 0:
                    po, pk = ctx.psum[t4][:], f"PB{t4}"
                else:
                    po, pk = ctx.pbig[:, 512 * t4:512 * (t4 + 1)], f"PBIG{t4}"
                fw.emit("vector", "tensor_scalar", ytl[:, 512 * half:512 * (half + 1)], po,
                        syt[:, 12 + t4:13 + t4], None, op0=ALU.mult,
                        ins=[pk, "syt_us"], outs=["ytl"])
                fw.emit("vector", "tensor_tensor", ytl[:, 512 * half:512 * (half + 1)],
                        ytl[:, 512 * half:512 * (half + 1)], xsl[:, 512 * half:512 * (half + 1)],
                        op=ALU.add, ins=["ytl", xkey], outs=["ytl"])
            fw.emit("vector", "reduce_sum", lnc[:, 0:1], ytl, axis=AX.X, ins=["ytl"], outs=["lnc_s"])
            fw.emit("scalar", "activation", ybuf[:, 0:1024], ytl, AF.Square,
                    accum_out=lnc[:, 1:2], ins=["ytl"], outs=["yb0", "yb1", "lnc_q"])
            fw.emit("vector", "tensor_scalar", lnc[:, 2:3], lnc[:, 0:1], 1.0 / 1024.0, None,
                    op0=ALU.mult, ins=["lnc_s"], outs=["lnc_mu"])
            fw.emit("vector", "tensor_scalar", lnc[:, 1:2], lnc[:, 1:2], 1.0 / 1024.0, None,
                    op0=ALU.mult, ins=["lnc_q"], outs=["lnc_q"])
            fw.emit("vector", "tensor_tensor", lnc[:, 3:4], lnc[:, 2:3], lnc[:, 2:3], op=ALU.mult,
                    ins=["lnc_mu"], outs=["lnc_rs"])
            fw.emit("vector", "tensor_tensor", lnc[:, 3:4], lnc[:, 1:2], lnc[:, 3:4], op=ALU.subtract,
                    ins=["lnc_q", "lnc_rs"], outs=["lnc_rs"])
            fw.emit("scalar", "activation", lnc[:, 3:4], lnc[:, 3:4], AF.Sqrt,
                    bias=hp.view("epscol")[:, 0:1], scale=1.0, ins=["lnc_rs", "epscol"], outs=["lnc_rs"])
            fw.emit("vector", "reciprocal", lnc[:, 3:4], lnc[:, 3:4], ins=["lnc_rs"], outs=["lnc_rs"])
            for half in range(2):
                fw.emit("vector", "tensor_scalar", ytl[:, 512 * half:512 * (half + 1)],
                        ytl[:, 512 * half:512 * (half + 1)], lnc[:, 2:3], lnc[:, 3:4],
                        op0=ALU.subtract, op1=ALU.mult, ins=["ytl", "lnc_mu", "lnc_rs"], outs=["ytl2"])
                x4_write(t, half, ytl[:, 512 * half:512 * (half + 1)], ["ytl2"])
    ctx.x4piece = x4piece



def build_program(dump=(), blocks=("l0a", "l0f", "l1a", "l1f")):
    nc = bass.Bass("TRN2", target_bir_lowering=False)
    ctx = Ctx(nc, dump=dump)
    load_constants(ctx)
    if "l0a" in blocks:
        build_l0_attn(ctx)
    if "l0f" in blocks:
        build_l0_ffn(ctx)
    if "l1a" in blocks:
        build_l1_attn(ctx)
    if "l1f" in blocks:
        build_l1_ffn(ctx)
    fw, hp = ctx.fw, ctx.heap
    o = ctx.dout("out", [T, D])
    g = fw.group("gpsimd")
    if hasattr(ctx, "x4piece"):
        for t in range(NTILE):
            for half in range(2):
                ap, key = ctx.x4piece[(t, half)]
                fw.dma("gpsimd", g, out=o[128 * t:128 * (t + 1), 512 * half:512 * (half + 1)],
                       in_=ap, ins=[key])
    else:
        src = hp.view("x")
        for t in range(NTILE):
            fw.dma("gpsimd", g, out=o[128 * t:128 * (t + 1), :], in_=src[:, 1024 * t:1024 * (t + 1)], ins=["x"])
    fw.finish("gpsimd")
    return nc, ctx


def kernel(**inputs):
    in_maps = shard_inputs(inputs)
    nc, ctx = build_program()
    res = run_bass_kernel_spmd(nc, in_maps, core_ids=list(range(8)))
    out = np.stack([r["out"] for r in res.results], axis=0)
    return out.astype(np.float32)


# version 02 (final)
# speedup vs baseline: 1.0280x; 1.0280x over previous
import numpy as np
import concourse.bass as bass
import concourse.mybir as mybir
from concourse.bass_utils import run_bass_kernel_spmd

F32 = mybir.dt.float32
U8 = mybir.dt.uint8
AF = mybir.ActivationFunctionType
ALU = mybir.AluOpType
AX = mybir.AxisListType

B, T, D = 8, 2048, 1024
NTILE = T // 128
NCHUNK = D // 128
BIGC = 16448
HEAP_COLS = 52430


COMPUTE_ENGINES = ("tensor", "vector", "scalar", "gpsimd")


class Fw:

    def __init__(self, nc, n_dma_sems=80):
        self.nc = nc
        self.sems = {}
        self.cnt = {}
        for e in COMPUTE_ENGINES:
            self.sems[e] = nc.semaphore(f"fws_{e}").__enter__()
            self.cnt[e] = 0
        self.dma_keys = {"hw": [], "sw": []}
        for i in range(n_dma_sems):
            pool = "sw" if i < n_dma_sems // 4 else "hw"
            key = f"dma_{pool}{i}"
            self.sems[key] = nc.semaphore(f"fwd_{i}").__enter__()
            self.cnt[key] = 0
            self.dma_keys[pool].append(key)
        self._dma_rr = {"hw": 0, "sw": 0}
        self.hwm = {}
        self.wr = {}
        self.rd = {}
        self.frozen = set()
        self.nops = 0

    def eng(self, name):
        return getattr(self.nc, name)

    def group(self, queue="sync"):
        pool = "sw" if queue == "gpsimd" else "hw"
        keys = self.dma_keys[pool]
        key = keys[self._dma_rr[pool] % len(keys)]
        self._dma_rr[pool] += 1
        return key

    def _gather(self, ins, outs, extra):
        toks = []
        for k in ins:
            t = self.wr.get(k)
            if t is not None:
                toks.append(t)
        for k in outs:
            t = self.wr.get(k)
            if t is not None:
                toks.append(t)
            toks.extend(self.rd.get(k, ()))
        toks.extend(t for t in extra if t is not None)
        return toks

    def _wait(self, waiter, toks):
        merged = {}
        for sk, c in toks:
            if sk.startswith("dma"):
                c = self.cnt[sk]
                self.frozen.add(sk)
            if c > 0 and merged.get(sk, 0) < c:
                merged[sk] = c
        for sk, c in sorted(merged.items()):
            if self.hwm.get((waiter, sk), 0) >= c:
                continue
            self.eng(waiter).wait_ge(self.sems[sk], c)
            self.hwm[(waiter, sk)] = c

    def _record(self, tok, ins, outs):
        for k in ins:
            self.rd.setdefault(k, []).append(tok)
        for k in outs:
            self.wr[k] = tok
            self.rd[k] = []

    def emit(self, engine, method, *args, ins=(), outs=(), extra=(), **kw):
        self._wait(engine, self._gather(ins, outs, extra))
        inst = getattr(self.eng(engine), method)(*args, **kw)
        self.cnt[engine] += 1
        inst.then_inc(self.sems[engine], 1)
        self.nops += 1
        tok = (engine, self.cnt[engine])
        self._record(tok, ins, outs)
        return tok

    def dma(self, queue, grp, *, out, in_, ins=(), outs=(), extra=(), allow_slow=False):
        if grp in self.frozen:
            c = self.cnt[grp]
            if self.hwm.get((queue, grp), 0) < c:
                self.eng(queue).wait_ge(self.sems[grp], c)
                self.hwm[(queue, grp)] = c
            self.frozen.discard(grp)
        self._wait(queue, self._gather(ins, outs, extra))
        if allow_slow:
            inst = self.eng(queue).dma_start(out=out, in_=in_, allow_slow_non_contiguous=True)
        else:
            inst = self.eng(queue).dma_start(out=out, in_=in_)
        self.cnt[grp] += 16
        inst.then_inc(self.sems[grp], 16)
        self.nops += 1
        tok = (grp, self.cnt[grp])
        self._record(tok, ins, outs)
        return tok

    def mm_group(self, psum_key, items, ins=(), n_banks=1):
        n = len(items)
        for i, (o, l, r) in enumerate(items):
            self.emit("tensor", "matmul", o, l, r, start=(i == 0), stop=(i == n - 1),
                      ins=ins, outs=([psum_key] if i == 0 else []),
                      extra=([] if i == 0 else [self.wr.get(psum_key)]))
        self.wr[psum_key] = ("tensor", self.cnt["tensor"])

    def alias(self, new_key, old_keys):
        toks = []
        for k in old_keys:
            t = self.wr.get(k)
            if t is not None:
                toks.append(t)
            toks.extend(self.rd.get(k, ()))
        if toks:
            self.rd[new_key] = self.rd.get(new_key, []) + toks

    def finish(self, waiter="gpsimd"):
        toks = [(k, v) for k, v in self.cnt.items() if v > 0]
        self._wait(waiter, toks)


class Heap:

    def __init__(self, nc, fw, cols=HEAP_COLS, name="heap"):
        self.ap = nc.sbuf_tensor(name, [128, cols], F32).__enter__()
        self.fw = fw
        self.cols = cols
        self.top = 0
        self.live = {}
        self.subs = {}
        self.holes = []

    def _merge(self):
        self.holes.sort(key=lambda h: h[0])
        out = []
        for h in self.holes:
            if out and out[-1][1] == h[0]:
                out[-1][1] = h[1]
                out[-1][2] = out[-1][2] + h[2]
            else:
                out.append(h)
        while out and out[-1][1] >= self.top:
            h = out.pop()
            self.top = h[0]
            self.top_keys = getattr(self, "top_keys", []) + h[2]
        self.holes = out

    def alloc(self, name, ncols, align=2, subs=()):
        ncols = (ncols + align - 1) // align * align
        self._merge()
        best = None
        for h in self.holes:
            s = (h[0] + align - 1) // align * align
            room = h[1] - s
            if room >= ncols and (best is None or (h[1] - h[0]) < (best[1] - best[0])):
                best = h
        if best is not None:
            h = best
            if ncols < 1024 and (h[1] - ncols) % align == 0:
                start, end = h[1] - ncols, h[1]
                h[1] = start
            else:
                start = (h[0] + align - 1) // align * align
                end = start + ncols
                h[0] = end
            self.fw.alias(name, h[2])
            for sub in subs:
                self.fw.alias(sub, h[2])
            if h[1] - h[0] <= 0:
                self.holes.remove(h)
            self.live[name] = (start, end)
            self.subs[name] = list(subs)
            return self.ap[:, start:end]
        start = (self.top + align - 1) // align * align
        end = start + ncols
        if end > self.cols:
            import sys
            print("HEAP STATE live:", sorted(self.live.items(), key=lambda kv: kv[1][0]), file=sys.stderr)
            print("holes:", [(h[0], h[1]) for h in self.holes], "top:", self.top, file=sys.stderr)
        assert end <= self.cols, f"heap overflow: {name} needs [{start},{end}) cap {self.cols}"
        self.top = end
        if getattr(self, "top_keys", None):
            self.fw.alias(name, self.top_keys)
            for sub in subs:
                self.fw.alias(sub, self.top_keys)
        self.live[name] = (start, end)
        self.subs[name] = list(subs)
        return self.ap[:, start:end]

    def free(self, name):
        s, e = self.live.pop(name)
        self.holes.append([s, e, [name] + self.subs.pop(name, [])])

    def view(self, name):
        s, e = self.live[name]
        return self.ap[:, s:e]



def _hilo(v64):
    hi = v64.astype(np.float32)
    lo = (v64 - hi.astype(np.float64)).astype(np.float32)
    return hi, lo


def _rope_tables_l0(T_, Dh=128, theta=10000.0):
    inv = 1.0 / theta ** (np.arange(0, Dh, 2, dtype=np.float64) / Dh)
    ang = np.arange(T_, dtype=np.float64)[:, None] * inv[None, :]
    cos = np.cos(ang)
    sin = np.sin(ang)
    cosf = np.concatenate([cos, cos], axis=1).T
    sinf = np.concatenate([-sin, sin], axis=1).T
    return cosf, sinf


def _xpos_tables(T_, Dh=64, window=512):
    half_idx = np.arange(0, Dh, 2, dtype=np.float64)
    inv = 1.0 / 10000.0 ** (half_idx / Dh)
    t = np.arange(T_, dtype=np.float64)
    ang = t[:, None] * inv[None, :]
    ang = np.concatenate([ang, ang], axis=1)
    base = (half_idx + 0.4 * Dh) / (1.4 * Dh)
    power = (t - T_ // 2) / float(window // 2)
    scale = base[None, :] ** power[:, None]
    scale = np.concatenate([scale, scale], axis=1)
    sign = np.concatenate([-np.ones(32), np.ones(32)])[None, :]
    tabs = {
        "xq_cos": np.cos(ang) * scale,
        "xq_sin": np.sin(ang) * scale * sign,
        "xk_cos": np.cos(ang) / scale,
        "xk_sin": np.sin(ang) / scale * sign,
    }
    out = {}
    for k, v in tabs.items():
        r = np.ascontiguousarray(v.reshape(NTILE, 128, 64).transpose(1, 0, 2).reshape(128, NTILE * 64))
        out[k], out[k + "_lo"] = _hilo(r)
    return out


def _l0f_masks():
    r = np.arange(128)[:, None]
    c = np.arange(640)[None, :]
    parts = [((c < r) | (c > r + 512)).astype(np.uint8)]
    for t_ in range(4):
        w = 128 * (t_ + 1)
        cc = np.arange(w)[None, :]
        parts.append((cc > 128 * t_ + r).astype(np.uint8))
    c5 = np.arange(512)[None, :]
    parts.append(((c5 < r) | (c5 > r + 256)).astype(np.uint8))
    parts.append(((c5 < r + 128) | (c5 > r + 384)).astype(np.uint8))
    return np.concatenate(parts, axis=1)


def host_constants():
    c = {}
    c["ident"] = np.eye(128, dtype=np.float32)
    p64 = np.zeros((128, 128), np.float32)
    for i in range(128):
        p64[(i + 64) % 128, i] = 1.0
    c["perm64"] = p64
    cos0, sin0 = _rope_tables_l0(T)
    c["cos0"], c["cos0_lo"] = _hilo(cos0)
    c["sin0"], c["sin0_lo"] = _hilo(sin0)
    rs128 = np.full((128, 2), 0.0, np.float64)
    v = 1.0 / np.sqrt(np.float64(128.0))
    hi = np.float32(v)
    c["rs128"] = np.ascontiguousarray(
        np.broadcast_to(np.array([hi, np.float32(v - np.float64(hi))], np.float32), (128, 2)))
    c["triu_u8"] = np.triu(np.ones((128, 128), np.uint8), k=1)
    c["ones1"] = np.ones((1, 128), np.float32)
    c.update(_xpos_tables(T))
    c["l0f_masks"] = _l0f_masks()
    return c


def _prep_l0_ffn(full, shared):
    f32 = lambda a: np.ascontiguousarray(a, dtype=np.float32)
    wqkv = f32(full["L0_ffn_wqkv"])
    shared["f_wqkv"] = np.ascontiguousarray(
        wqkv.reshape(2, 8, 128, 3, 512).transpose(0, 3, 2, 1, 4))
    shared["f_wo"] = f32(full["L0_ffn_wo"])
    shared["f_lng"] = np.ascontiguousarray(f32(full["L0_ffn_ln_g"]).reshape(2, 8, 128).transpose(0, 2, 1))
    shared["f_lnb"] = np.ascontiguousarray(f32(full["L0_ffn_ln_b"]).reshape(2, 8, 128).transpose(0, 2, 1))
    shared["f_qs"] = f32(full["L0_ffn_q_scale"])
    shared["f_ks"] = f32(full["L0_ffn_k_scale"])
    w1 = f32(full["L0_ffn_w1"])
    b1 = f32(full["L0_ffn_b1"])
    w1p = np.zeros((2, 22, 128, 8, 2, 128), np.float32)
    b1c = np.zeros((2, 128, 44), np.float32)
    for i in range(22):
        w = min(128, 2730 - 128 * i)
        for h2 in range(2):
            sl = w1[:, :, 2730 * h2 + 128 * i: 2730 * h2 + 128 * i + w]
            w1p[:, i, :, :, h2, :w] = sl.reshape(2, 8, 128, w).transpose(0, 2, 1, 3)
            b1c[:, :w, 22 * h2 + i] = b1[:, 2730 * h2 + 128 * i: 2730 * h2 + 128 * i + w]
    shared["f_w1p"] = np.ascontiguousarray(w1p)
    shared["f_b1c"] = b1c
    w2 = f32(full["L0_ffn_w2"])
    w2p = np.zeros((2, 22, 128, 1024), np.float32)
    for i in range(22):
        w = min(128, 2730 - 128 * i)
        w2p[:, i, :w, :] = w2[:, 128 * i:128 * i + w, :]
    shared["f_w2p"] = np.ascontiguousarray(w2p)
    shared["f_b2"] = f32(full["L0_ffn_b2"])


def shard_inputs(full):
    consts = host_constants()
    f32 = lambda a: np.ascontiguousarray(a, dtype=np.float32)
    x = f32(full["L0_attn_hidden_states"])
    wq = f32(full["L0_attn_Wq"])
    wk = f32(full["L0_attn_Wk"])
    wv = f32(full["L0_attn_Wv"])
    shared = dict(consts)
    shared["a_wq"] = np.ascontiguousarray(wq.reshape(8, 128, 8, 128).transpose(2, 1, 0, 3))
    shared["a_wk"] = np.ascontiguousarray(wk.reshape(8, 128, 4, 128).transpose(2, 1, 0, 3))
    shared["a_wv"] = np.ascontiguousarray(wv.reshape(8, 128, 4, 128).transpose(2, 1, 0, 3))
    shared["a_wo"] = f32(full["L0_attn_Wo"])
    _prep_l0_ffn(full, shared)
    wq1 = f32(full["L1_attn_Wq"]).reshape(8, 128, 1024)
    wk1 = f32(full["L1_attn_Wk"]).reshape(8, 128, 1024)
    shared["g_wqk"] = np.ascontiguousarray(np.concatenate([wq1, wk1], axis=2))
    shared["g_wv"] = np.ascontiguousarray(f32(full["L1_attn_Wv"]).reshape(8, 128, 1024))
    shared["g_bqk"] = np.ascontiguousarray(np.concatenate(
        [f32(full["L1_attn_bq"]).reshape(8, 128).T, f32(full["L1_attn_bk"]).reshape(8, 128).T], axis=1))
    shared["g_bv"] = f32(full["L1_attn_bv"]).reshape(1, 1024)
    shared["h_w1T"] = np.ascontiguousarray(f32(full["L1_ffn_w1"]).T)
    shared["h_w2T"] = np.ascontiguousarray(f32(full["L1_ffn_w2"]).T)
    cw = f32(full["L1_ffn_conv_w"])[:, 0, :]
    shared["h_ccol"] = np.ascontiguousarray(cw.reshape(32, 128, 3).transpose(1, 2, 0).reshape(128, 96))
    shared["h_bcol"] = np.ascontiguousarray(f32(full["L1_ffn_conv_b"]).reshape(32, 128).T)
    in_maps = []
    for b in range(B):
        m = dict(shared)
        m["x_in"] = f32(x[b])
        m["xT_in"] = np.ascontiguousarray(x[b].T)
        in_maps.append(m)
    return in_maps



class Ctx:
    def __init__(self, nc, dump=()):
        self.nc = nc
        self.fw = Fw(nc)
        self.heap = Heap(nc, self.fw)
        self.dump = set(dump)
        self.dram = {}
        self.psum = []
        for i in range(4):
            self.psum.append(nc.psum_tensor(f"pb{i}", [128, 512], F32).__enter__())
        self.pbig = nc.psum_tensor("pbig", [128, 2048], F32).__enter__()
        self.tri = nc.sbuf_tensor("tri_u8", [128, 128], U8).__enter__()
        self.masks = nc.sbuf_tensor("masks_u8", [128, 2944], U8).__enter__()

    def din(self, name, shape, dtype=F32):
        if name not in self.dram:
            self.dram[name] = self.nc.dram_tensor(name, list(shape), dtype, kind="ExternalInput").ap()
        return self.dram[name]

    def dout(self, name, shape, dtype=F32):
        self.dram[name] = self.nc.dram_tensor(name, list(shape), dtype, kind="ExternalOutput").ap()
        return self.dram[name]

    def dump_raw(self, name, src_ap, keys, rows=128):
        if name not in self.dump:
            return
        cols = src_ap.shape[-1]
        o = self.dout("dump_" + name, [rows, cols])
        g = self.fw.group("gpsimd")
        self.fw.dma("gpsimd", g, out=o[:, :], in_=src_ap, ins=list(keys))

    def maybe_dump(self, key, name=None):
        if key not in self.dump:
            return
        fw, hp = self.fw, self.heap
        o = self.dout("dump_" + (name or key), [T, D])
        g = fw.group("gpsimd")
        src = hp.view(key)
        for t in range(NTILE):
            fw.dma("gpsimd", g, out=o[128 * t:128 * (t + 1), :],
                   in_=src[:, 1024 * t:1024 * (t + 1)], ins=[key])


def load_constants(ctx):
    fw, hp = ctx.fw, ctx.heap
    g = fw.group()
    ident = hp.alloc("ident", 128)
    perm64 = hp.alloc("perm64", 128)
    negbig = hp.alloc("negbig", 640)
    fw.dma("sync", g, out=ident, in_=ctx.din("ident", [128, 128]), outs=["ident"])
    fw.dma("sync", g, out=perm64, in_=ctx.din("perm64", [128, 128]), outs=["perm64"])
    fw.dma("sync", g, out=ctx.tri[:], in_=ctx.din("triu_u8", [128, 128], U8), outs=["triu"])
    fw.emit("gpsimd", "memset", negbig, -1.0e37, outs=["negbig"])
    eps = hp.alloc("epscol", 2)
    fw.emit("gpsimd", "memset", eps, 1e-5, outs=["epscol"])
    onesr = hp.alloc("ones1", 128)
    fw.dma("sync", g, out=onesr[0:1, :], in_=ctx.din("ones1", [1, 128]), outs=["ones1"])
    fw.dma("sync", g, out=ctx.masks[:], in_=ctx.din("l0f_masks", [128, 2944], U8), outs=["masks"])


def postln(ctx, y_key, out_key, resid_key=None):
    fw, hp = ctx.fw, ctx.heap
    y = hp.view(y_key)
    out = hp.view(out_key)
    nt = NTILE
    st = hp.alloc("ln_st", 4 * nt, subs=("ln_s1", "ln_sq", "ln_mu", "ln_rs"))
    s1, sq = st[:, 0:nt], st[:, nt:2 * nt]
    mu, rs = st[:, 2 * nt:3 * nt], st[:, 3 * nt:4 * nt]
    fw.alias("PBIG0", [])
    for t in range(nt):
        yt = y[:, 1024 * t:1024 * (t + 1)]
        fw.emit("vector", "reduce_sum", s1[:, t:t + 1], yt, axis=AX.X, ins=[y_key], outs=["ln_s1"])
        fw.emit("scalar", "activation", ctx.pbig[:, 0:1024], yt, AF.Square,
                accum_out=sq[:, t:t + 1], ins=[y_key], outs=["PBIG0", "PBIG1", "ln_sq"])
    inv = 1.0 / 1024.0
    fw.emit("vector", "tensor_scalar", mu, s1, inv, None, op0=ALU.mult, ins=["ln_s1"], outs=["ln_mu"])
    fw.emit("vector", "tensor_scalar", sq, sq, inv, None, op0=ALU.mult, ins=["ln_sq"], outs=["ln_sq"])
    fw.emit("vector", "tensor_tensor", rs, mu, mu, op=ALU.mult, ins=["ln_mu"], outs=["ln_rs"])
    fw.emit("vector", "tensor_tensor", rs, sq, rs, op=ALU.subtract, ins=["ln_sq", "ln_rs"], outs=["ln_rs"])
    fw.emit("scalar", "activation", rs, rs, AF.Sqrt, bias=hp.view("epscol")[:, 0:1], scale=1.0,
            ins=["ln_rs", "epscol"], outs=["ln_rs"])
    fw.emit("vector", "reciprocal", rs, rs, ins=["ln_rs"], outs=["ln_rs"])
    for t in range(nt):
        fw.emit("vector", "tensor_scalar", out[:, 1024 * t:1024 * (t + 1)], y[:, 1024 * t:1024 * (t + 1)],
                mu[:, t:t + 1], rs[:, t:t + 1], op0=ALU.subtract, op1=ALU.mult,
                ins=[y_key, "ln_mu", "ln_rs"], outs=[out_key])
    hp.free("ln_st")


def rope_l0(ctx, raw_key, out_key, pT, nblocks=4):
    fw, hp = ctx.fw, ctx.heap
    raw = hp.view(raw_key)
    out = hp.view(out_key)
    swp = hp.view("swp")
    for b in range(nblocks):
        sl = slice(512 * b, 512 * (b + 1))
        fw.emit("tensor", "matmul", pT[:], hp.view("perm64"), raw[:, sl],
                start=True, stop=True, ins=["perm64", raw_key], outs=["PB1"])
        fw.emit("vector", "tensor_tensor", swp, pT[:], hp.view("sin0")[:, sl], op=ALU.mult,
                ins=["PB1", "sin0"], outs=["swp"])
        fw.emit("vector", "tensor_tensor", out[:, sl], raw[:, sl], hp.view("cos0")[:, sl], op=ALU.mult,
                ins=[raw_key, "cos0"], outs=[out_key])
        fw.emit("vector", "tensor_tensor", out[:, sl], out[:, sl], swp, op=ALU.add,
                ins=["swp", out_key], outs=[out_key])


def build_l0_attn(ctx):
    nc, fw, hp = ctx.nc, ctx.fw, ctx.heap
    NKV = 4

    xT = hp.alloc("xT", BIGC)
    aout = hp.alloc("aout", BIGC)
    cos0 = hp.alloc("cos0", 2048)
    sin0 = hp.alloc("sin0", 2048)
    kr = hp.alloc("kr", 2048)
    vv = hp.alloc("vv", 2048)
    qr = hp.alloc("qr", 2048)
    work = hp.alloc("work", 2560, subs=("kraw", "swp", "qraw"))
    attnT = hp.alloc("attnT", 256, subs=("attnT0", "attnT1"))
    wk = hp.alloc("wk", 1024)
    wvb = hp.alloc("wvb", 1024)
    wqb = hp.alloc("wqb", 1024)
    sm = hp.alloc("sm", 8, subs=("sm0", "sm1", "sm2", "sm3"))
    kraw = work[:, 0:2048]
    attn = work[:, 0:2048]
    ws = hp.live["work"][0]
    hp.live["kraw"] = (ws, ws + 2048)
    hp.live["swp"] = (ws + 2048, ws + 2560)

    gx = fw.group()
    for c in range(NCHUNK):
        fw.dma("sync", gx, out=xT[:, 2048 * c:2048 * (c + 1)],
               in_=ctx.din("xT_in", [D, T])[128 * c:128 * (c + 1), :], outs=["xT"])
    gt = fw.group()
    fw.dma("sync", gt, out=cos0, in_=ctx.din("cos0", [128, 2048]), outs=["cos0"])
    fw.dma("sync", gt, out=sin0, in_=ctx.din("sin0", [128, 2048]), outs=["sin0"])
    fw.dma("sync", gt, out=kraw, in_=ctx.din("cos0_lo", [128, 2048]), outs=["kraw"])
    fw.emit("vector", "tensor_tensor", cos0, cos0, kraw, op=ALU.add, ins=["cos0", "kraw"], outs=["cos0"])
    fw.dma("sync", gt, out=kraw, in_=ctx.din("sin0_lo", [128, 2048]), outs=["kraw"])
    fw.emit("vector", "tensor_tensor", sin0, sin0, kraw, op=ALU.add, ins=["sin0", "kraw"], outs=["sin0"])
    rs128 = hp.alloc("rs128", 2)
    fw.dma("sync", gt, out=rs128, in_=ctx.din("rs128", [128, 2]), outs=["rs128"])
    fw.emit("vector", "tensor_tensor", rs128[:, 0:1], rs128[:, 0:1], rs128[:, 1:2], op=ALU.add,
            ins=["rs128"], outs=["rs128c"])

    pK, pT, pO, pV = ctx.psum[0], ctx.psum[1], ctx.psum[2], ctx.psum[3]
    pS = ctx.pbig

    for g in range(NKV):
        gk = fw.group()
        fw.dma("sync", gk, out=wk, in_=ctx.din("a_wk", [4, 128, 8, 128])[g].rearrange("p c f -> p (c f)"), outs=["wk"])
        fw.dma("sync", gk, out=wvb, in_=ctx.din("a_wv", [4, 128, 8, 128])[g].rearrange("p c f -> p (c f)"), outs=["wv"])
        for b in range(4):
            fw.mm_group("PB0", [(pK[:], wk[:, 128 * c:128 * (c + 1)],
                                 xT[:, 2048 * c + 512 * b:2048 * c + 512 * (b + 1)]) for c in range(NCHUNK)],
                        ins=["wk", "xT"])
            fw.emit("vector", "tensor_copy", kraw[:, 512 * b:512 * (b + 1)], pK[:], ins=["PB0"], outs=["kraw"])
        rope_l0(ctx, "kraw", "kr", pT)
        for t in range(NTILE):
            fw.mm_group("PB3", [(pV[:, 0:128], xT[:, 2048 * c + 128 * t:2048 * c + 128 * (t + 1)],
                                 wvb[:, 128 * c:128 * (c + 1)]) for c in range(NCHUNK)],
                        ins=["wv", "xT"])
            fw.emit("vector", "tensor_copy", vv[:, 128 * t:128 * (t + 1)], pV[:, 0:128], ins=["PB3"], outs=["vv"])
        for h in (2 * g, 2 * g + 1):
            gq = fw.group()
            fw.dma("sync", gq, out=wqb, in_=ctx.din("a_wq", [8, 128, 8, 128])[h].rearrange("p c f -> p (c f)"), outs=["wq"])
            for b in range(4):
                fw.mm_group("PB0", [(pK[:], wqb[:, 128 * c:128 * (c + 1)],
                                     xT[:, 2048 * c + 512 * b:2048 * c + 512 * (b + 1)]) for c in range(NCHUNK)],
                            ins=["wq", "xT"])
                fw.emit("vector", "tensor_copy", kraw[:, 512 * b:512 * (b + 1)], pK[:], ins=["PB0"], outs=["kraw"])
            rope_l0(ctx, "kraw", "qr", pT)
            for t in range(NTILE):
                nk = 128 * (t + 1)
                nkb = (nk + 511) // 512
                pdeps = []
                for kb in range(nkb):
                    n = min(512, nk - 512 * kb)
                    fw.emit("tensor", "matmul", pS[:, 512 * kb:512 * kb + n],
                            qr[:, 128 * t:128 * (t + 1)], kr[:, 512 * kb:512 * kb + n],
                            start=True, stop=True, ins=["qr", "kr"], outs=[f"PBIG{kb}"])
                    pdeps.append(f"PBIG{kb}")
                dkey = f"PBIG{(nk - 128) // 512}"
                fw.emit("vector", "copy_predicated", pS[:, nk - 128:nk], ctx.tri[:],
                        hp.view("negbig")[:, 0:128], ins=["triu", "negbig", dkey], outs=[dkey])
                fw.emit("vector", "tensor_reduce", sm[:, 0:1], pS[:, 0:nk], axis=AX.X, op=ALU.max,
                        negate=True, ins=pdeps, outs=["sm0"])
                fw.emit("vector", "tensor_tensor", sm[:, 1:2], sm[:, 0:1], rs128[:, 0:1],
                        op=ALU.mult, ins=["sm0", "rs128c"], outs=["sm1"])
                fw.emit("scalar", "activation", attn[:, 0:nk], pS[:, 0:nk], AF.Exp,
                        bias=sm[:, 1:2], scale=rs128[:, 0:1], accum_out=sm[:, 2:3],
                        ins=pdeps + ["sm1", "rs128c"], outs=["kraw", "sm2"])
                fw.emit("vector", "reciprocal", sm[:, 3:4], sm[:, 2:3], ins=["sm2"], outs=["sm3"])
                fw.emit("vector", "tensor_scalar", attn[:, 0:nk], attn[:, 0:nk], sm[:, 3:4], None,
                        op0=ALU.mult, ins=["kraw", "sm3"], outs=["kraw"])
                first = True
                for kb in range(t + 1):
                    a_sl = attnT[:, 128 * (kb % 2):128 * (kb % 2 + 1)]
                    akey = f"attnT{kb % 2}"
                    fw.emit("tensor", "transpose", pT[:, 0:128], attn[:, 128 * kb:128 * (kb + 1)],
                            hp.view("ident"), ins=["kraw", "ident"], outs=["PB1"])
                    fw.emit("vector", "tensor_copy", a_sl, pT[:, 0:128], ins=["PB1"], outs=[akey])
                    fw.emit("tensor", "matmul", pO[:, 0:128], vv[:, 128 * kb:128 * (kb + 1)], a_sl,
                            start=first, stop=(kb == t), ins=["vv", akey],
                            outs=(["PB2"] if first else []), extra=([] if first else [fw.wr.get("PB2")]))
                    first = False
                fw.wr["PB2"] = ("tensor", fw.cnt["tensor"])
                fw.emit("vector", "tensor_copy", aout[:, 2048 * h + 128 * t:2048 * h + 128 * (t + 1)],
                        pO[:, 0:128], ins=["PB2"], outs=["aout"])

    for nm in ("kr", "vv", "qr", "work", "attnT", "wk", "wvb", "wqb", "sm", "cos0", "sin0", "rs128"):
        hp.free(nm)
    hp.live.pop("swp", None)
    hp.live.pop("kraw", None)

    hp.free("xT")
    y = hp.alloc("y", BIGC)
    wo = hp.alloc("wof", 8192)
    xb = hp.alloc("xb", 2048, subs=("xb0", "xb1"))
    gw2 = fw.group()
    for c in range(NCHUNK):
        fw.dma("sync", gw2, out=wo[:, 1024 * c:1024 * (c + 1)],
               in_=ctx.din("a_wo", [D, D])[128 * c:128 * (c + 1), :], outs=["wof"])
    gxr = fw.group()
    for t in range(NTILE):
        xs = xb[:, 1024 * (t % 2):1024 * (t % 2 + 1)]
        xkey = f"xb{t % 2}"
        fw.dma("sync", gxr, out=xs, in_=ctx.din("x_in", [T, D])[128 * t:128 * (t + 1), :], outs=[xkey])
        for half in range(2):
            fw.mm_group(f"PBIG{half}",
                        [(ctx.pbig[:, 512 * half:512 * (half + 1)],
                          hp.view("aout")[:, 2048 * c + 128 * t:2048 * c + 128 * (t + 1)],
                          wo[:, 1024 * c + 512 * half:1024 * c + 512 * (half + 1)]) for c in range(NCHUNK)],
                        ins=["aout", "wof"])
            fw.emit("vector", "tensor_tensor",
                    y[:, 1024 * t + 512 * half:1024 * t + 512 * (half + 1)],
                    ctx.pbig[:, 512 * half:512 * (half + 1)],
                    xs[:, 512 * half:512 * (half + 1)], op=ALU.add,
                    ins=[f"PBIG{half}", xkey], outs=["y"])
    hp.free("aout")
    hp.free("perm64")
    x1 = hp.alloc("x", BIGC)
    postln(ctx, "y", "x")
    hp.free("wof")
    hp.free("xb")
    hp.free("y")
    ctx.maybe_dump("x", "x1")


def ap_bc(sl, reps, width):
    return bass.AP(sl.tensor, sl.offset, [list(sl.ap[0]), [0, reps], [1, width]])


def ap_swap32(sl, heads):
    return bass.AP(sl.tensor, sl.offset + 32, [list(sl.ap[0]), [64, heads], [-32, 2], [1, 32]])


def transpose_tm_fm(ctx, src_key, dst_key, dst_ap, shifted, src_tiles=NTILE,
                    evac=None):
    fw, hp = ctx.fw, ctx.heap
    CW = 2056 if shifted else 2048
    src = hp.view(src_key)
    if shifted:
        for c in range(4, 8):
            fw.emit("gpsimd", "memset", dst_ap[:, CW * c:CW * c + 1], 0.0, outs=[dst_key])
    for c in range(NCHUNK):
        off = 1 if (shifted and c >= 4) else 0
        for t in range(src_tiles):
            pb = ctx.psum[t % 2]
            pk = f"PB{t % 2}"
            fw.emit("tensor", "transpose", pb[:, 0:128], src[:, 1024 * t + 128 * c:1024 * t + 128 * (c + 1)],
                    hp.view("ident"), ins=[src_key, "ident"], outs=[pk])
            dsl = dst_ap[:, CW * c + off + 128 * t: CW * c + off + 128 * (t + 1)]
            if evac is None:
                fw.emit("vector", "tensor_copy", dsl, pb[:, 0:128], ins=[pk], outs=[dst_key])
            else:
                evac(c, pb[:, 0:128], dsl, pk, dst_key)


def spill_residual(ctx, dram_name):
    fw, hp = ctx.fw, ctx.heap
    if dram_name not in ctx.dram:
        ctx.dram[dram_name] = ctx.nc.dram_tensor(dram_name, [T, D], F32).ap()
    d = ctx.dram[dram_name]
    g = fw.group("gpsimd")
    src = hp.view("x")
    for t in range(NTILE):
        fw.dma("gpsimd", g, out=d[128 * t:128 * (t + 1), :], in_=src[:, 1024 * t:1024 * (t + 1)],
               ins=["x"], outs=[dram_name + f"_t{t}"])


def build_l0_ffn(ctx):
    nc, fw, hp = ctx.nc, ctx.fw, ctx.heap
    spill_residual(ctx, "spill_l0f")
    for l in range(2):
        _l0f_layer(ctx, l)
    yf = hp.alloc("yfin", BIGC)
    xb = hp.alloc("xbf", 2048, subs=("xbf0", "xbf1"))
    d = ctx.dram["spill_l0f"]
    g = fw.group()
    for t in range(NTILE):
        xs = xb[:, 1024 * (t % 2):1024 * (t % 2 + 1)]
        xkey = f"xbf{t % 2}"
        fw.dma("sync", g, out=xs, in_=d[128 * t:128 * (t + 1), :],
               ins=[f"spill_l0f_t{t}"], outs=[xkey])
        fw.emit("vector", "tensor_tensor", yf[:, 1024 * t:1024 * (t + 1)],
                hp.view("x")[:, 1024 * t:1024 * (t + 1)], xs, op=ALU.add,
                ins=["x", xkey], outs=["yfin"])
    postln(ctx, "yfin", "x")
    hp.free("yfin")
    hp.free("xbf")
    ctx.maybe_dump("x", "x2")


def _qk_tile(ctx, psum, pkey, A_key, B_key, t, out512, out_key, sskey):
    fw, hp = ctx.fw, ctx.heap
    st = hp.view(sskey)
    ss, rn = st[:, 0:8], st[:, 16:24]
    for h in range(8):
        fw.emit("scalar", "activation", ctx.psum[3][:, 0:64], psum[:, 64 * h:64 * (h + 1)], AF.Square,
                accum_out=ss[:, h:h + 1], ins=[pkey], outs=["PB3", f"{sskey}_ss"])
    fw.emit("scalar", "activation", st[:, 8:16], ss, AF.Sqrt, bias=0.0, scale=1.0,
            ins=[f"{sskey}_ss"], outs=[f"{sskey}_n"])
    fw.emit("vector", "tensor_scalar", st[:, 8:16], st[:, 8:16], 1e-12, None, op0=ALU.max,
            ins=[f"{sskey}_n"], outs=[f"{sskey}_n"])
    fw.emit("vector", "reciprocal", rn, st[:, 8:16], ins=[f"{sskey}_n"], outs=[f"{sskey}_r"])
    u = hp.view("w512")
    for h in range(8):
        fw.emit("vector", "tensor_scalar", u[:, 64 * h:64 * (h + 1)], psum[:, 64 * h:64 * (h + 1)],
                rn[:, h:h + 1], None, op0=ALU.mult, ins=[pkey, f"{sskey}_r"], outs=["w512"])
    A = hp.view(A_key)[:, 64 * t:64 * (t + 1)]
    Bt = hp.view(B_key)[:, 64 * t:64 * (t + 1)]
    t512 = hp.view("t512")
    t3 = t512.rearrange("p (h f) -> p h f", f=64)
    fw.emit("vector", "tensor_tensor", t3, ap_swap32(u, 8), ap_bc(Bt, 8, 64), op=ALU.mult,
            ins=["w512", B_key], outs=["t512"])
    o3 = out512.rearrange("p (h f) -> p h f", f=64)
    u3 = u.rearrange("p (h f) -> p h f", f=64)
    fw.emit("vector", "tensor_tensor", o3, u3, ap_bc(A, 8, 64), op=ALU.mult,
            ins=["w512", A_key], outs=[out_key])
    fw.emit("vector", "tensor_tensor", out512, out512, t512, op=ALU.add,
            ins=["t512", out_key], outs=[out_key])


def _l0f_layer(ctx, l):
    nc, fw, hp = ctx.nc, ctx.fw, ctx.heap
    CW = 2056
    spill_residual(ctx, "spill_tmp")
    hp.free("x")
    xTs = hp.alloc("xTs", 8 * CW)
    xrb = hp.alloc("xrb", 2048, subs=("xrb0", "xrb1"))
    for c in range(4, 8):
        fw.emit("gpsimd", "memset", xTs[:, CW * c:CW * c + 1], 0.0, outs=["xTs"])
    d0 = ctx.dram["spill_tmp"]
    gr = fw.group()
    for t in range(NTILE):
        xs = xrb[:, 1024 * (t % 2):1024 * (t % 2 + 1)]
        xkey = f"xrb{t % 2}"
        fw.dma("sync", gr, out=xs, in_=d0[128 * t:128 * (t + 1), :],
               ins=[f"spill_tmp_t{t}"], outs=[xkey])
        for c in range(NCHUNK):
            off = 1 if c >= 4 else 0
            pb = ctx.psum[c % 2]
            pk = f"PB{c % 2}"
            fw.emit("tensor", "transpose", pb[:, 0:128], xs[:, 128 * c:128 * (c + 1)],
                    hp.view("ident"), ins=[xkey, "ident"], outs=[pk])
            fw.emit("vector", "tensor_copy", xTs[:, CW * c + off + 128 * t:CW * c + off + 128 * (t + 1)],
                    pb[:, 0:128], ins=[pk], outs=["xTs"])
    hp.free("xrb")

    raw = hp.alloc("xraw", 4096)
    rawlo = hp.alloc("xrawlo", 4096)
    g = fw.group()
    for i, nm in enumerate(("xq_cos", "xq_sin", "xk_cos", "xk_sin")):
        fw.dma("sync", g, out=raw[:, 1024 * i:1024 * (i + 1)], in_=ctx.din(nm, [128, 1024]), outs=["xraw"])
        fw.dma("sync", g, out=rawlo[:, 1024 * i:1024 * (i + 1)], in_=ctx.din(nm + "_lo", [128, 1024]), outs=["xrawlo"])
    fw.emit("vector", "tensor_tensor", raw, raw, rawlo, op=ALU.add, ins=["xraw", "xrawlo"], outs=["xraw"])
    qsr = hp.alloc("qsr", 128)
    fw.dma("sync", g, out=qsr[0:1, 0:64], in_=ctx.din("f_qs", [2, 64])[l:l + 1, :], outs=["qsr"])
    fw.dma("sync", g, out=qsr[0:1, 64:128], in_=ctx.din("f_ks", [2, 64])[l:l + 1, :], outs=["qsr"])
    ABq = hp.alloc("ABq", 2048)
    ABk = hp.alloc("ABk", 2048)
    qb = hp.alloc("qb", 256)
    for j, (base, swap) in enumerate([(0, False), (0, True), (64, False), (64, True)]):
        if swap:
            rhs = bass.AP(qsr.tensor, qsr.offset + base + 32, [[qsr.ap[0][0], 1], [-32, 2], [1, 32]])
        else:
            rhs = qsr[0:1, base:base + 64]
        fw.emit("tensor", "matmul", ctx.psum[0][:, 0:64], hp.view("ones1")[0:1, :], rhs,
                start=True, stop=True, ins=["ones1", "qsr"], outs=["PB0"])
        fw.emit("vector", "tensor_copy", qb[:, 64 * j:64 * (j + 1)], ctx.psum[0][:, 0:64],
                ins=["PB0"], outs=["qb"])
    for j in range(4):
        dst = ABq if j < 2 else ABk
        dk = "ABq" if j < 2 else "ABk"
        a3 = dst[:, 1024 * (j % 2):1024 * (j % 2 + 1)].rearrange("p (t f) -> p t f", f=64)
        r3 = raw[:, 1024 * j:1024 * (j + 1)].rearrange("p (t f) -> p t f", f=64)
        fw.emit("vector", "tensor_tensor", a3, r3, ap_bc(qb[:, 64 * j:64 * (j + 1)], NTILE, 64),
                op=ALU.mult, ins=["xraw", "qb"], outs=[dk])
    hp.free("xraw")
    hp.free("xrawlo")
    hp.free("qb")
    hp.free("qsr")
    for k, src, off in (("Aq", "ABq", 0), ("Bq", "ABq", 1024), ("Ak", "ABk", 0), ("Bk", "ABk", 1024)):
        hp.live[k] = (hp.live[src][0] + off, hp.live[src][0] + off + 1024)
        fw.alias(k, [src])
        fw.wr[k] = fw.wr.get(src)

    krT = hp.alloc("krT", 8192)
    vbuf = hp.alloc("vbuf", 8192)
    qrT = hp.alloc("qrT", 8192)
    wthird = hp.alloc("wthird", 4096)
    w512 = hp.alloc("w512", 512)
    t512 = hp.alloc("t512", 512)
    p512 = hp.alloc("p512", 512)
    sst = hp.alloc("sst", 24, subs=("sst_ss", "sst_n", "sst_r"))
    smf = hp.alloc("smf", 8, subs=("smf0", "smf1", "smf2", "smf3"))

    wq = ctx.din("f_wqkv", [2, 3, 128, 8, 512])

    def load_third(j):
        gg = fw.group()
        fw.dma("sync", gg, out=wthird, in_=wq[l, j].rearrange("p c f -> p (c f)"), outs=["wthird"])

    def qkv_tile(t):
        fw.mm_group("PB2", [(ctx.psum[2][:], xTs[:, CW * c + 128 * t:CW * c + 128 * (t + 1)],
                             wthird[:, 512 * c:512 * (c + 1)]) for c in range(NCHUNK)],
                    ins=["xTs", "wthird"])

    load_third(1)
    for t in range(NTILE):
        qkv_tile(t)
        _qk_tile(ctx, ctx.psum[2][:], "PB2", "Ak", "Bk", t, hp.view("p512"), "p512", "sst")
        for j in range(4):
            fw.emit("tensor", "transpose", ctx.psum[0][:, 0:128], hp.view("p512")[:, 128 * j:128 * (j + 1)],
                    hp.view("ident"), ins=["p512", "ident"], outs=["PB0"])
            fw.emit("vector", "tensor_copy", krT[:, 2048 * j + 128 * t:2048 * j + 128 * (t + 1)],
                    ctx.psum[0][:, 0:128], ins=["PB0"], outs=["krT"])
    hp.free("ABk")
    hp.live.pop("Ak", None)
    hp.live.pop("Bk", None)
    load_third(2)
    for t in range(NTILE):
        qkv_tile(t)
        fw.emit("vector", "tensor_copy", vbuf[:, 512 * t:512 * (t + 1)], ctx.psum[2][:],
                ins=["PB2"], outs=["vbuf"])
    load_third(0)
    for t in range(NTILE):
        qkv_tile(t)
        _qk_tile(ctx, ctx.psum[2][:], "PB2", "Aq", "Bq", t, hp.view("p512"), "p512", "sst")
        for j in range(4):
            fw.emit("tensor", "transpose", ctx.psum[0][:, 0:128], hp.view("p512")[:, 128 * j:128 * (j + 1)],
                    hp.view("ident"), ins=["p512", "ident"], outs=["PB0"])
            fw.emit("vector", "tensor_copy", qrT[:, 2048 * j + 128 * t:2048 * j + 128 * (t + 1)],
                    ctx.psum[0][:, 0:128], ins=["PB0"], outs=["qrT"])
    hp.free("xTs")
    hp.free("wthird")
    hp.free("ABq")
    for k in ("Aq", "Bq"):
        hp.live.pop(k, None)

    oT = hp.alloc("oT", 8192)
    fat = hp.alloc("fat", 640)
    at2 = hp.alloc("at2", 256, subs=("at2_0", "at2_1"))
    masks = ctx.masks
    for t in range(NTILE):
        kLo = max(0, 128 * t - 512)
        width = 128 * (t + 1) - kLo
        if t >= 4:
            msl = masks[:, 0:640]
        else:
            off = 640 + sum(128 * (j + 1) for j in range(t))
            msl = masks[:, off:off + width]
        otok = hp.view("p512")
        for h in range(8):
            j, half = h // 2, h % 2
            lq = qrT[64 * half:64 * half + 64, 2048 * j + 128 * t:2048 * j + 128 * (t + 1)]
            pdeps = []
            for s0 in range(0, width, 512):
                n = min(512, width - s0)
                lk = krT[64 * half:64 * half + 64, 2048 * j + kLo + s0:2048 * j + kLo + s0 + n]
                bk = f"PBIG{s0 // 512}"
                fw.emit("tensor", "matmul", ctx.pbig[:, s0:s0 + n], lq, lk,
                        start=True, stop=True, ins=["qrT", "krT"], outs=[bk])
                pdeps.append(bk)
            fw.emit("vector", "copy_predicated", ctx.pbig[:, 0:width], msl, hp.view("negbig")[:, 0:width],
                    ins=["masks", "negbig"] + pdeps, outs=pdeps)
            fw.emit("vector", "tensor_reduce", smf[:, 0:1], ctx.pbig[:, 0:width], axis=AX.X,
                    op=ALU.max, negate=True, ins=pdeps, outs=["smf0"])
            fw.emit("vector", "tensor_scalar", smf[:, 1:2], smf[:, 0:1], 8.0, None, op0=ALU.mult,
                    ins=["smf0"], outs=["smf1"])
            fw.emit("scalar", "activation", fat[:, 0:width], ctx.pbig[:, 0:width], AF.Exp,
                    bias=smf[:, 1:2], scale=8.0, accum_out=smf[:, 2:3],
                    ins=pdeps + ["smf1"], outs=["fat", "smf2"])
            fw.emit("vector", "reciprocal", smf[:, 3:4], smf[:, 2:3], ins=["smf2"], outs=["smf3"])
            fw.emit("vector", "tensor_scalar", fat[:, 0:width], fat[:, 0:width], smf[:, 3:4], None,
                    op0=ALU.mult, ins=["fat", "smf3"], outs=["fat"])
            first = True
            nkb = width // 128
            for kb in range(nkb):
                aslot = at2[:, 128 * (kb % 2):128 * (kb % 2 + 1)]
                akey = f"at2_{kb % 2}"
                fw.emit("tensor", "transpose", ctx.psum[0][:, 0:128], fat[:, 128 * kb:128 * (kb + 1)],
                        hp.view("ident"), ins=["fat", "ident"], outs=["PB0"])
                fw.emit("vector", "tensor_copy", aslot, ctx.psum[0][:, 0:128], ins=["PB0"], outs=[akey])
                tv = kLo // 128 + kb
                fw.emit("tensor", "matmul", ctx.psum[2][:, 0:64], aslot,
                        vbuf[:, 512 * tv + 64 * h:512 * tv + 64 * (h + 1)],
                        start=first, stop=(kb == nkb - 1), ins=[akey, "vbuf"],
                        outs=(["PB2"] if first else []), extra=([] if first else [fw.wr.get("PB2")]))
                first = False
            fw.wr["PB2"] = ("tensor", fw.cnt["tensor"])
            fw.emit("vector", "tensor_copy", otok[:, 64 * h:64 * (h + 1)], ctx.psum[2][:, 0:64],
                    ins=["PB2"], outs=["p512"])
        for j in range(4):
            fw.emit("tensor", "transpose", ctx.psum[0][:, 0:128], otok[:, 128 * j:128 * (j + 1)],
                    hp.view("ident"), ins=["p512", "ident"], outs=["PB0"])
            fw.emit("vector", "tensor_copy", oT[:, 2048 * j + 128 * t:2048 * j + 128 * (t + 1)],
                    ctx.psum[0][:, 0:128], ins=["PB0"], outs=["oT"])
    ctx.dump_raw(f"krT{l}", krT, ["krT"])
    ctx.dump_raw(f"qrT{l}", qrT, ["qrT"])
    ctx.dump_raw(f"v{l}", vbuf, ["vbuf"])
    ctx.dump_raw(f"oT{l}", oT, ["oT"])
    hp.free("qrT")
    hp.free("krT")
    hp.free("vbuf")

    wo = hp.alloc("wol", 4096)
    g = fw.group()
    fw.dma("sync", g, out=wo, in_=ctx.din("f_wo", [2, 512, D])[l].rearrange("(c p) f -> p c f", p=128), outs=["wol"])
    xnew = hp.alloc("x", BIGC)
    xb = hp.alloc("xb2", 2048, subs=("xb2_0", "xb2_1"))
    d = ctx.dram["spill_tmp"]
    gx = fw.group()
    for t in range(NTILE):
        xs = xb[:, 1024 * (t % 2):1024 * (t % 2 + 1)]
        xkey = f"xb2_{t % 2}"
        fw.dma("sync", gx, out=xs, in_=d[128 * t:128 * (t + 1), :], ins=[f"spill_tmp_t{t}"], outs=[xkey])
        for half in range(2):
            fw.mm_group(f"PBIG{half}",
                        [(ctx.pbig[:, 512 * half:512 * (half + 1)],
                          oT[:, 2048 * c + 128 * t:2048 * c + 128 * (t + 1)],
                          wo[:, 1024 * c + 512 * half:1024 * c + 512 * (half + 1)]) for c in range(4)],
                        ins=["oT", "wol"])
            fw.emit("vector", "tensor_tensor", xnew[:, 1024 * t + 512 * half:1024 * t + 512 * (half + 1)],
                    ctx.pbig[:, 512 * half:512 * (half + 1)], xs[:, 512 * half:512 * (half + 1)],
                    op=ALU.add, ins=[f"PBIG{half}", xkey], outs=["x"])
    hp.free("oT")
    hp.free("wol")
    hp.free("xb2")

    gcol = hp.alloc("gcol", 16)
    gg = fw.group()
    fw.dma("sync", gg, out=gcol[:, 0:8], in_=ctx.din("f_lng", [2, 128, 8])[l], outs=["gcol"])
    fw.dma("sync", gg, out=gcol[:, 8:16], in_=ctx.din("f_lnb", [2, 128, 8])[l], outs=["gcol"])
    stt = hp.alloc("stt", 168, subs=("stt_s1", "stt_s2", "stt_q1", "stt_q2",
                                     "stt_s2s", "stt_q2s", "stt_mu", "stt_rs",
                                     "stt_mun", "stt_rsn", "stt_t", "v0"))
    tok0 = fw.emit("vector", "memset", stt, 0.0, outs=["stt"])
    for sub in ("stt_s1", "stt_s2", "stt_q1", "stt_q2", "stt_s2s", "stt_q2s",
                "stt_mu", "stt_rs", "stt_mun", "stt_rsn", "stt_t", "v0"):
        fw.wr[sub] = tok0
    xv = hp.view("x")
    for t in range(NTILE):
        fw.emit("vector", "reduce_sum", stt[:, t:t + 1], xv[:, 1024 * t:1024 * t + 512], axis=AX.X,
                ins=["x"], outs=["stt_s1"])
        fw.emit("vector", "reduce_sum", stt[:, 16 + t:17 + t], xv[:, 1024 * t + 512:1024 * (t + 1)], axis=AX.X,
                ins=["x"], outs=["stt_s2"])
        fw.emit("scalar", "activation", ctx.psum[3][:], xv[:, 1024 * t:1024 * t + 512], AF.Square,
                accum_out=stt[:, 32 + t:33 + t], ins=["x"], outs=["PB3", "stt_q1"])
        fw.emit("scalar", "activation", ctx.psum[3][:], xv[:, 1024 * t + 512:1024 * (t + 1)], AF.Square,
                accum_out=stt[:, 48 + t:49 + t], ins=["x"], outs=["PB3", "stt_q2"])
    gs = fw.group("gpsimd")
    fw.dma("gpsimd", gs, out=stt[1:128, 64:80], in_=stt[0:127, 16:32], ins=["stt_s2"], outs=["stt_s2s"])
    fw.dma("gpsimd", gs, out=stt[0:1, 65:80], in_=stt[127:128, 16:31], ins=["stt_s2"], outs=["stt_s2s"])
    fw.dma("gpsimd", gs, out=stt[1:128, 80:96], in_=stt[0:127, 32 + 16:48 + 16], ins=["stt_q2"], outs=["stt_q2s"])
    fw.dma("gpsimd", gs, out=stt[0:1, 81:96], in_=stt[127:128, 48:63], ins=["stt_q2"], outs=["stt_q2s"])
    inv = 1.0 / 1024.0
    mu, rs = stt[:, 96:112], stt[:, 112:128]
    fw.emit("vector", "tensor_tensor", mu, stt[:, 0:16], stt[:, 64:80], op=ALU.add,
            ins=["stt_s1", "stt_s2s"], outs=["stt_mu"])
    fw.emit("vector", "tensor_scalar", mu, mu, inv, None, op0=ALU.mult, ins=["stt_mu"], outs=["stt_mu"])
    fw.emit("vector", "tensor_tensor", rs, stt[:, 32:48], stt[:, 80:96], op=ALU.add,
            ins=["stt_q1", "stt_q2s"], outs=["stt_rs"])
    fw.emit("vector", "tensor_scalar", rs, rs, inv, None, op0=ALU.mult, ins=["stt_rs"], outs=["stt_rs"])
    fw.emit("vector", "tensor_tensor", stt[:, 128:144], mu, mu, op=ALU.mult, ins=["stt_mu"], outs=["stt_t"])
    fw.emit("vector", "tensor_tensor", rs, rs, stt[:, 128:144], op=ALU.subtract,
            ins=["stt_rs", "stt_t"], outs=["stt_rs"])
    fw.emit("scalar", "activation", rs, rs, AF.Sqrt, bias=hp.view("epscol")[:, 0:1], scale=1.0,
            ins=["stt_rs", "epscol"], outs=["stt_rs"])
    fw.emit("vector", "reciprocal", rs, rs, ins=["stt_rs"], outs=["stt_rs"])
    gs2 = fw.group("gpsimd")
    fw.alias("stt_mun", ["stt_t"])
    fw.dma("gpsimd", gs2, out=stt[0:127, 128:144], in_=mu[1:128, :], ins=["stt_mu"], outs=["stt_mun"])
    fw.dma("gpsimd", gs2, out=stt[127:128, 128:143], in_=mu[0:1, 1:16], ins=["stt_mu"], outs=["stt_mun"])
    fw.dma("gpsimd", gs2, out=stt[0:127, 144:160], in_=rs[1:128, :], ins=["stt_rs"], outs=["stt_rsn"])
    fw.dma("gpsimd", gs2, out=stt[127:128, 144:159], in_=rs[0:1, 1:16], ins=["stt_rs"], outs=["stt_rsn"])
    mun, rsn = stt[:, 128:144], stt[:, 144:160]

    CW = 2056
    hnTs = hp.alloc("hnTs", 8 * CW)
    zb = hp.view("w512")
    for t in range(NTILE):
        fw.emit("vector", "tensor_scalar", zb, xv[:, 1024 * t:1024 * t + 512], mu[:, t:t + 1],
                rs[:, t:t + 1], op0=ALU.subtract, op1=ALU.mult,
                ins=["x", "stt_mu", "stt_rs"], outs=["w512"])
        for c in range(4):
            fw.emit("tensor", "transpose", ctx.psum[0][:, 0:128], zb[:, 128 * c:128 * (c + 1)],
                    hp.view("ident"), ins=["w512", "ident"], outs=["PB0"])
            fw.emit("vector", "tensor_scalar", hnTs[:, CW * c + 128 * t:CW * c + 128 * (t + 1)],
                    ctx.psum[0][:, 0:128], gcol[:, c:c + 1], gcol[:, 8 + c:9 + c],
                    op0=ALU.mult, op1=ALU.add, ins=["PB0", "gcol"], outs=["hnTs"])
        fw.emit("vector", "tensor_scalar", zb, xv[:, 1024 * t + 512:1024 * (t + 1)], mun[:, t:t + 1],
                rsn[:, t:t + 1], op0=ALU.subtract, op1=ALU.mult,
                ins=["x", "stt_mun", "stt_rsn"], outs=["w512"])
        for c in range(4):
            fw.emit("tensor", "transpose", ctx.psum[0][:, 0:128], zb[:, 128 * c:128 * (c + 1)],
                    hp.view("ident"), ins=["w512", "ident"], outs=["PB0"])
            fw.emit("vector", "tensor_scalar", hnTs[:, CW * (c + 4) + 1 + 128 * t:CW * (c + 4) + 1 + 128 * (t + 1)],
                    ctx.psum[0][:, 0:128], gcol[:, 4 + c:5 + c], gcol[:, 12 + c:13 + c],
                    op0=ALU.mult, op1=ALU.add, ins=["PB0", "gcol"], outs=["hnTs"])
    fw.emit("vector", "tensor_tensor", stt[0:1, 160:161], mu[0:1, 0:1], rs[0:1, 0:1], op=ALU.mult,
            ins=["stt_mu", "stt_rs"], outs=["v0"])
    fw.emit("vector", "tensor_scalar", stt[0:1, 160:161], stt[0:1, 160:161], -1.0, None, op0=ALU.mult,
            ins=["v0"], outs=["v0"])
    fw.emit("tensor", "matmul", ctx.psum[0][:, 0:1], hp.view("ones1")[0:1, :], stt[0:1, 160:161],
            start=True, stop=True, ins=["ones1", "v0"], outs=["PB0"])
    for c in range(4):
        fw.emit("vector", "tensor_scalar", hnTs[:, CW * (c + 4):CW * (c + 4) + 1],
                ctx.psum[0][:, 0:1], gcol[:, 4 + c:5 + c], gcol[:, 12 + c:13 + c],
                op0=ALU.mult, op1=ALU.add, ins=["PB0", "gcol"], outs=["hnTs"])

    for k in ("p512", "fat", "at2", "sst", "smf"):
        hp.free(k)
    hp.live.pop("at2_0", None)
    b1c = hp.alloc("b1c", 44)
    gb = fw.group()
    fw.dma("sync", gb, out=b1c, in_=ctx.din("f_b1c", [2, 128, 44])[l], outs=["b1c"])
    b2bc = hp.alloc("b2bc", 1024)
    b2r = hp.alloc("b2r", 1024)
    fw.dma("sync", gb, out=b2r[0:1, :], in_=ctx.din("f_b2", [2, D])[l:l + 1, :], outs=["b2r"])
    for half in range(2):
        fw.emit("tensor", "matmul", ctx.psum[0][:], hp.view("ones1")[0:1, :], b2r[0:1, 512 * half:512 * (half + 1)],
                start=True, stop=True, ins=["ones1", "b2r"], outs=["PB0"])
        fw.emit("vector", "tensor_copy", b2bc[:, 512 * half:512 * (half + 1)], ctx.psum[0][:],
                ins=["PB0"], outs=["b2bc"])
    hp.free("b2r")
    pbufA = hp.alloc("pbufA", 16 * 512, subs=tuple(f"pbuf{i}" for i in range(16)))
    pbufB = hp.alloc("pbufB", 6 * 512, subs=tuple(f"pbuf{i}" for i in range(16, 22)))
    pbuf_sl = lambda i: (pbufA[:, 512 * i:512 * (i + 1)] if i < 16
                         else pbufB[:, 512 * (i - 16):512 * (i - 15)])
    w1s = hp.alloc("w1s", 2048)
    w2s = hp.alloc("w2s", 1024)
    abuf = hp.view("w512")
    gbuf = hp.view("t512")
    w1d = ctx.din("f_w1p", [2, 22, 128, 8, 2, 128])
    w2d = ctx.din("f_w2p", [2, 22, 128, 1024])
    for blk in range(4):
        for i in range(22):
            wsl = w1s
            wkey = "w1s"
            gw = fw.group()
            fw.dma("sync", gw, out=wsl, in_=w1d[l, i].rearrange("p c h f -> p (c h f)"), outs=[wkey])
            fw.mm_group("PBIG0", [(ctx.pbig[:, 0:512], wsl[:, 256 * c:256 * c + 128],
                                   hnTs[:, CW * c + 512 * blk:CW * c + 512 * (blk + 1)]) for c in range(NCHUNK)],
                        ins=[wkey, "hnTs"])
            fw.mm_group("PBIG1", [(ctx.pbig[:, 512:1024], wsl[:, 256 * c + 128:256 * c + 256],
                                   hnTs[:, CW * c + 512 * blk:CW * c + 512 * (blk + 1)]) for c in range(NCHUNK)],
                        ins=[wkey, "hnTs"])
            fw.emit("vector", "tensor_scalar", abuf, ctx.pbig[:, 0:512], b1c[:, i:i + 1], None,
                    op0=ALU.add, ins=["PBIG0", "b1c"], outs=["w512"])
            fw.emit("scalar", "activation", gbuf, ctx.pbig[:, 512:1024], AF.Gelu,
                    bias=b1c[:, 22 + i:23 + i], scale=1.0, ins=["PBIG1", "b1c"], outs=["t512"])
            fw.emit("vector", "tensor_tensor", pbuf_sl(i), abuf, gbuf, op=ALU.mult,
                    ins=["w512", "t512"], outs=[f"pbuf{i}"])
        for i in range(22):
            wsl = w2s
            wkey = "w2s"
            gw = fw.group()
            fw.dma("sync", gw, out=wsl, in_=w2d[l, i], outs=[wkey])
            for t4 in range(4):
                for half in range(2):
                    if half == 0:
                        po, pk = ctx.psum[t4][:], f"PB{t4}"
                    else:
                        po, pk = ctx.pbig[:, 512 * t4:512 * (t4 + 1)], f"PBIG{t4}"
                    fw.emit("tensor", "matmul", po,
                            pbuf_sl(i)[:, 128 * t4:128 * (t4 + 1)],
                            wsl[:, 512 * half:512 * (half + 1)],
                            start=(i == 0), stop=(i == 21),
                            ins=[f"pbuf{i}", wkey],
                            outs=([pk] if i == 0 else []),
                            extra=([] if i == 0 else [fw.wr.get(pk)]))
                    if i == 21:
                        fw.wr[pk] = ("tensor", fw.cnt["tensor"])
        for t4 in range(4):
            t = 4 * blk + t4
            for half in range(2):
                if half == 0:
                    po, pk = ctx.psum[t4][:], f"PB{t4}"
                else:
                    po, pk = ctx.pbig[:, 512 * t4:512 * (t4 + 1)], f"PBIG{t4}"
                sl = slice(1024 * t + 512 * half, 1024 * t + 512 * (half + 1))
                fw.emit("vector", "tensor_tensor", gbuf, po, b2bc[:, 512 * half:512 * (half + 1)],
                        op=ALU.add, ins=[pk, "b2bc"], outs=["t512"])
                fw.emit("vector", "tensor_tensor", xv[:, sl], gbuf, xv[:, sl], op=ALU.add,
                        ins=["t512", "x"], outs=["x"])
    for k in ("hnTs", "pbufA", "pbufB", "w1s", "w2s", "b1c", "b2bc", "gcol", "stt",
              "w512", "t512"):
        hp.free(k)


def build_l1_attn(ctx):
    nc, fw, hp = ctx.nc, ctx.fw, ctx.heap
    spill_residual(ctx, "spill_l1a")
    hp.free("x")

    y = hp.alloc("y", BIGC)
    kT = hp.alloc("kT", 6144, subs=tuple(f"kTs{s}" for s in range(3)))
    vr = hp.alloc("vr", 6144, subs=tuple(f"vr{s}" for s in range(6)))
    qT = hp.alloc("qT", 4096, subs=("qT0", "qT1"))
    xTb = hp.alloc("xTb", 2048)
    xrb = hp.alloc("xrb", 4096, subs=tuple(f"xrbt{j}" for j in range(4)))
    wqk = hp.alloc("wqk", 4096, subs=("wqk0", "wqk1"))
    wvs = hp.alloc("wvs", 2048, subs=("wvs0", "wvs1"))
    bqk = hp.alloc("bqk", 16)
    bvbc = hp.alloc("bvbc", 1024)
    afat = hp.alloc("afat", 512)
    at2 = hp.alloc("at2", 256, subs=("at2_0", "at2_1"))
    sm = hp.alloc("smL", 8, subs=("smL0", "smL1", "smL2", "smL3"))

    g0 = fw.group()
    fw.dma("sync", g0, out=bqk[:, 0:8], in_=ctx.din("g_bqk", [128, 16])[:, 0:8], outs=["bqk"])
    fw.dma("sync", g0, out=bqk[:, 8:16], in_=ctx.din("g_bqk", [128, 16])[:, 8:16], outs=["bqk"])
    bvr = hp.alloc("bvr", 1024)
    fw.dma("sync", g0, out=bvr[0:1, :], in_=ctx.din("g_bv", [1, D]), outs=["bvr"])
    for half in range(2):
        fw.emit("tensor", "matmul", ctx.psum[0][:], hp.view("ones1")[0:1, :],
                bvr[0:1, 512 * half:512 * (half + 1)], start=True, stop=True,
                ins=["ones1", "bvr"], outs=["PB0"])
        fw.emit("vector", "tensor_copy", bvbc[:, 512 * half:512 * (half + 1)], ctx.psum[0][:],
                ins=["PB0"], outs=["bvbc"])
    hp.free("bvr")

    d = ctx.dram["spill_l1a"]
    qkd = ctx.din("g_wqk", [8, 128, 2048])
    wvd = ctx.din("g_wv", [8, 128, 1024])
    NB = 8

    def projections(b):
        gr = fw.group()
        tkeys = []
        for j in range(2):
            t = 2 * b + j
            sl = xrb[:, 1024 * (t % 4):1024 * (t % 4 + 1)]
            tk = f"xrbt{t % 4}"
            fw.dma("sync", gr, out=sl, in_=d[128 * t:128 * (t + 1), :],
                   ins=[f"spill_l1a_t{t}"], outs=[tk])
            tkeys.append(tk)
            for c in range(NCHUNK):
                pb = ctx.psum[c % 2]
                pk = f"PB{c % 2}"
                fw.emit("tensor", "transpose", pb[:, 0:128], sl[:, 128 * c:128 * (c + 1)],
                        hp.view("ident"), ins=[tk, "ident"], outs=[pk])
                fw.emit("vector", "tensor_copy", xTb[:, 256 * c + 128 * j:256 * c + 128 * (j + 1)],
                        pb[:, 0:128], ins=[pk], outs=["xTb"])
        qsl = 2048 * (b % 2)
        qkey = f"qT{b % 2}"
        for c in range(NCHUNK):
            wsl = wqk[:, 2048 * (c % 2):2048 * (c % 2 + 1)]
            wk_ = f"wqk{c % 2}"
            gw = fw.group()
            fw.dma("sync", gw, out=wsl, in_=qkd[c], outs=[wk_])
            for gsl in range(NCHUNK):
                st = (c == 0 and gsl % 2 == 0)
                sp = (c == NCHUNK - 1 and gsl % 2 == 1)
                fw.emit("tensor", "matmul", ctx.pbig[:, 256 * gsl:256 * (gsl + 1)],
                        wsl[:, 128 * gsl:128 * (gsl + 1)], xTb[:, 256 * c:256 * (c + 1)],
                        start=st, stop=sp,
                        ins=[wk_, "xTb"], outs=([f"PBIG{gsl // 2}"] if st else []),
                        extra=([] if st else [fw.wr.get(f"PBIG{gsl // 2}")]))
                if sp:
                    fw.wr[f"PBIG{gsl // 2}"] = ("tensor", fw.cnt["tensor"])
                fw.emit("tensor", "matmul", ctx.psum[gsl // 2][:, 256 * (gsl % 2):256 * (gsl % 2 + 1)],
                        wsl[:, 1024 + 128 * gsl:1024 + 128 * (gsl + 1)], xTb[:, 256 * c:256 * (c + 1)],
                        start=st, stop=sp,
                        ins=[wk_, "xTb"], outs=([f"PB{gsl // 2}"] if st else []),
                        extra=([] if st else [fw.wr.get(f"PB{gsl // 2}")]))
                if sp:
                    fw.wr[f"PB{gsl // 2}"] = ("tensor", fw.cnt["tensor"])
        s = b % 3
        for gsl in range(NCHUNK):
            fw.emit("vector", "tensor_scalar", qT[:, qsl + 256 * gsl:qsl + 256 * (gsl + 1)],
                    ctx.pbig[:, 256 * gsl:256 * (gsl + 1)], bqk[:, gsl:gsl + 1], None,
                    op0=ALU.add, ins=[f"PBIG{gsl // 2}", "bqk"], outs=[qkey])
            fw.emit("vector", "tensor_scalar", kT[:, 768 * gsl + 256 * s:768 * gsl + 256 * (s + 1)],
                    ctx.psum[gsl // 2][:, 256 * (gsl % 2):256 * (gsl % 2 + 1)],
                    bqk[:, 8 + gsl:9 + gsl], None,
                    op0=ALU.add, ins=[f"PB{gsl // 2}", "bqk"], outs=[f"kTs{s}"])
        for c in range(NCHUNK):
            wsl = wvs[:, 1024 * (c % 2):1024 * (c % 2 + 1)]
            wk_ = f"wvs{c % 2}"
            gw = fw.group()
            fw.dma("sync", gw, out=wsl, in_=wvd[c], outs=[wk_])
            for j in range(2):
                for half in range(2):
                    pk = f"PBIG{2 + j}" if half == 0 else f"PB{2 + j}"
                    po = ctx.pbig[:, 1024 + 512 * j:1536 + 512 * j] if half == 0 else ctx.psum[2 + j][:]
                    fw.emit("tensor", "matmul", po,
                            xTb[:, 256 * c + 128 * j:256 * c + 128 * (j + 1)],
                            wsl[:, 512 * half:512 * (half + 1)],
                            start=(c == 0), stop=(c == NCHUNK - 1),
                            ins=[wk_, "xTb"], outs=([pk] if c == 0 else []),
                            extra=([] if c == 0 else [fw.wr.get(pk)]))
                    if c == NCHUNK - 1:
                        fw.wr[pk] = ("tensor", fw.cnt["tensor"])
        for j in range(2):
            t = 2 * b + j
            rs = t % 6
            for half in range(2):
                pk = f"PBIG{2 + j}" if half == 0 else f"PB{2 + j}"
                po = ctx.pbig[:, 1024 + 512 * j:1536 + 512 * j] if half == 0 else ctx.psum[2 + j][:]
                fw.emit("vector", "tensor_tensor", vr[:, 1024 * rs + 512 * half:1024 * rs + 512 * (half + 1)],
                        po, bvbc[:, 512 * half:512 * (half + 1)], op=ALU.add,
                        ins=[pk, "bvbc"], outs=[f"vr{rs}"])

    def attention_now(b):
        kdep = [f"kTs{bb % 3}" for bb in range(max(0, b - 1), min(NB, b + 2))]
        qsl = 2048 * (b % 2)
        qkey = f"qT{b % 2}"
        for jq in range(2):
            t = 2 * b + jq
            pieces = []
            if b > 0:
                pieces.append((0, (b - 1) % 3, 128, 128))
            pieces.append((128, b % 3, 0, 256))
            if b < NB - 1:
                pieces.append((384, (b + 1) % 3, 0, 128))
            items = []
            for (pc, s, so, w) in pieces:
                for c in range(NCHUNK):
                    items.append((ctx.pbig[:, pc:pc + w],
                                  qT[:, qsl + 256 * c + 128 * jq:qsl + 256 * c + 128 * (jq + 1)],
                                  kT[:, 768 * c + 256 * s + so:768 * c + 256 * s + so + w]))
            fw.mm_group("PBIG0", items, ins=[qkey] + kdep)
            if b == 0 and jq == 0:
                fw.emit("vector", "tensor_copy", ctx.pbig[:, 0:128], hp.view("negbig")[:, 0:128],
                        ins=["negbig"], outs=["PBIG0"], extra=[fw.wr.get("PBIG0")])
            if b == 0 and jq == 1:
                fw.emit("vector", "tensor_copy", ctx.pbig[:, 0:128], hp.view("negbig")[:, 0:128],
                        ins=["negbig"], outs=["PBIG0"], extra=[fw.wr.get("PBIG0")])
            if b == NB - 1:
                fw.emit("vector", "tensor_copy", ctx.pbig[:, 384:512], hp.view("negbig")[:, 0:128],
                        ins=["negbig"], outs=["PBIG0"], extra=[fw.wr.get("PBIG0")])
            moff = 1920 + 512 * jq
            fw.emit("vector", "copy_predicated", ctx.pbig[:, 0:512], ctx.masks[:, moff:moff + 512],
                    hp.view("negbig")[:, 0:512], ins=["masks", "negbig", "PBIG0"], outs=["PBIG0"])
            fw.emit("vector", "tensor_reduce", sm[:, 0:1], ctx.pbig[:, 0:512], axis=AX.X,
                    op=ALU.max, negate=True, ins=["PBIG0"], outs=["smL0"])
            fw.emit("vector", "tensor_scalar", sm[:, 1:2], sm[:, 0:1], 0.03125, None,
                    op0=ALU.mult, ins=["smL0"], outs=["smL1"])
            fw.emit("scalar", "activation", afat[:, 0:512], ctx.pbig[:, 0:512], AF.Exp,
                    bias=sm[:, 1:2], scale=0.03125, accum_out=sm[:, 2:3],
                    ins=["PBIG0", "smL1"], outs=["afat", "smL2"])
            fw.emit("vector", "reciprocal", sm[:, 3:4], sm[:, 2:3], ins=["smL2"], outs=["smL3"])
            fw.emit("vector", "tensor_scalar", afat[:, 0:512], afat[:, 0:512], sm[:, 3:4], None,
                    op0=ALU.mult, ins=["afat", "smL3"], outs=["afat"])
            first = True
            for kb in range(4):
                tv = 2 * b - 1 + kb
                if tv < 0 or tv >= 16:
                    continue
                aslot = at2[:, 128 * (kb % 2):128 * (kb % 2 + 1)]
                akey = f"at2_{kb % 2}"
                fw.emit("tensor", "transpose", ctx.psum[0][:, 0:128], afat[:, 128 * kb:128 * (kb + 1)],
                        hp.view("ident"), ins=["afat", "ident"], outs=["PB0"])
                fw.emit("vector", "tensor_copy", aslot, ctx.psum[0][:, 0:128], ins=["PB0"], outs=[akey])
                for half in range(2):
                    pk = f"PBIG{2 + half}"
                    fw.emit("tensor", "matmul", ctx.pbig[:, 1024 + 512 * half:1536 + 512 * half],
                            aslot, vr[:, 1024 * (tv % 6) + 512 * half:1024 * (tv % 6) + 512 * (half + 1)],
                            start=first, stop=(kb == 3 or (b == NB - 1 and kb == 2)),
                            ins=[akey, f"vr{tv % 6}"], outs=([pk] if first else []),
                            extra=([] if first else [fw.wr.get(pk)]))
                    if kb == 3 or (b == NB - 1 and kb == 2):
                        fw.wr[pk] = ("tensor", fw.cnt["tensor"])
                first = False
            for half in range(2):
                sl = slice(1024 * t + 512 * half, 1024 * t + 512 * (half + 1))
                fw.emit("vector", "tensor_tensor", y[:, sl],
                        ctx.pbig[:, 1024 + 512 * half:1536 + 512 * half],
                        xrb[:, 1024 * (t % 4) + 512 * half:1024 * (t % 4) + 512 * (half + 1)],
                        op=ALU.add, ins=[f"PBIG{2 + half}", f"xrbt{t % 4}"], outs=["y"])

    for b in range(NB):
        projections(b)
        if b >= 1:
            attention_now(b - 1)
    attention_now(NB - 1)

    for k in ("kT", "vr", "qT", "xTb", "xrb", "wqk", "wvs", "bqk", "bvbc", "afat", "at2", "smL"):
        hp.free(k)
    xn = hp.alloc("x", BIGC)
    postln(ctx, "y", "x")
    hp.free("y")
    ctx.maybe_dump("x", "x3")


def emit_round(ctx, v_ap, v_keys, out_ap, out_key, qs1, qs2, k1, k2):
    fw = ctx.fw
    fw.emit("vector", "tensor_scalar", qs1, v_ap, 128.5, None, op0=ALU.add,
            ins=list(v_keys), outs=[k1])
    fw.emit("vector", "tensor_copy", qs2, qs1, ins=[k1], outs=[k2])
    for p in (128, 64, 32, 16, 8, 4, 2, 1):
        fw.emit("vector", "tensor_scalar", out_ap, qs2, float(p), None, op0=ALU.is_ge,
                ins=[k2], outs=[out_key])
        fw.emit("vector", "scalar_tensor_tensor", qs2, out_ap, float(-p), qs2,
                op0=ALU.mult, op1=ALU.add, ins=[out_key, k2], outs=[k2])
    fw.emit("vector", "scalar_tensor_tensor", out_ap, qs2, -1.0, qs1,
            op0=ALU.mult, op1=ALU.add, ins=[k1, k2], outs=[out_key])
    fw.emit("vector", "tensor_scalar", out_ap, out_ap, -128.0, None, op0=ALU.add,
            ins=[out_key], outs=[out_key])


def build_l1_ffn(ctx):
    nc, fw, hp = ctx.nc, ctx.fw, ctx.heap
    spill_residual(ctx, "spill_l1f")
    CH, NI = 4096, 32

    st = hp.alloc("qst", 64, subs=("qst_m", "qst_s", "qst_r", "qst_re"))
    q1 = hp.alloc("qs1", 1024)
    q2 = hp.alloc("qs2", 1024)
    xv = hp.view("x")
    for t in range(NTILE):
        fw.emit("vector", "tensor_reduce", st[:, t:t + 1], xv[:, 1024 * t:1024 * (t + 1)],
                axis=AX.X, op=ALU.max, apply_absolute_value=True, ins=["x"], outs=["qst_m"])
    fw.emit("vector", "tensor_scalar", st[:, 0:16], st[:, 0:16], 1e-5, None, op0=ALU.max,
            ins=["qst_m"], outs=["qst_m"])
    fw.emit("vector", "reciprocal", st[:, 16:32], st[:, 0:16], ins=["qst_m"], outs=["qst_s"])
    fw.emit("vector", "tensor_scalar", st[:, 16:32], st[:, 16:32], 127.0, None, op0=ALU.mult,
            ins=["qst_s"], outs=["qst_s"])
    fw.emit("vector", "tensor_scalar", st[:, 32:48], st[:, 0:16], 1.0 / 127.0, None, op0=ALU.mult,
            ins=["qst_m"], outs=["qst_r"])
    for t in range(NTILE):
        sl = xv[:, 1024 * t:1024 * (t + 1)]
        fw.emit("vector", "tensor_scalar", sl, sl, st[:, 16 + t:17 + t], None, op0=ALU.mult,
                ins=["x", "qst_s"], outs=["x"])
        emit_round(ctx, sl, ["x"], sl, "x", q1, q2, "rq1", "rq2")
    ixT = hp.alloc("ixT", BIGC)
    for c in range(NCHUNK):
        for t in range(NTILE):
            pb = ctx.psum[t % 2]
            pk = f"PB{t % 2}"
            fw.emit("tensor", "transpose", pb[:, 0:128], xv[:, 1024 * t + 128 * c:1024 * t + 128 * (c + 1)],
                    hp.view("ident"), ins=["x", "ident"], outs=[pk])
            fw.emit("vector", "tensor_copy", ixT[:, 2048 * c + 128 * t:2048 * c + 128 * (t + 1)],
                    pb[:, 0:128], ins=[pk], outs=["ixT"])
    hp.free("x")
    hp.free("qs1")
    hp.free("qs2")

    wq1s = nc.dram_tensor("wq1s", [D, CH], F32).ap()
    wq2s = nc.dram_tensor("wq2s", [CH, D], F32).ap()
    w1d = ctx.din("h_w1T", [D, CH])
    w2d = ctx.din("h_w2T", [CH, D])
    wt = hp.alloc("wt", 4096, subs=("wt0",))
    wb1 = hp.alloc("wb1", 4096)
    wb2 = hp.alloc("wb2", 4096)
    wsc = hp.alloc("wsc", 16, subs=("acc", "onescol", "wtot", "thrs", "wred", "mw", "mwj"))
    fw.emit("gpsimd", "memset", wsc[:, 1:2], 1.0, outs=["onescol"])
    for wi, (wd, wqs, nch, nel) in enumerate([(w1d, wq1s, NCHUNK, D * CH), (w2d, wq2s, NI, CH * D)]):
        fw.emit("gpsimd", "memset", wsc[:, 0:1], 0.0, outs=["acc"])
        width = CH if wi == 0 else D
        for c in range(nch):
            g = fw.group()
            fw.dma("sync", g, out=wt[:, 0:width], in_=wd[128 * c:128 * (c + 1), :], outs=["wt0"])
            fw.emit("vector", "tensor_reduce", wsc[:, 8:9], wt[:, 0:width], axis=AX.X, op=ALU.add,
                    apply_absolute_value=True, ins=["wt0"], outs=["wred"])
            fw.emit("vector", "tensor_tensor", wsc[:, 0:1], wsc[:, 0:1], wsc[:, 8:9], op=ALU.add,
                    ins=["acc", "wred"], outs=["acc"])
        fw.emit("tensor", "matmul", ctx.psum[1][0:1, 0:1], wsc[:, 0:1], wsc[:, 1:2],
                start=True, stop=True, ins=["acc", "onescol"], outs=["PB1"])
        fw.emit("vector", "tensor_scalar", wsc[0:1, 9:10], ctx.psum[1][0:1, 0:1], 1.0 / nel, 1e-5,
                op0=ALU.mult, op1=ALU.max, ins=["PB1"], outs=["mw"])
        base = 2 + 3 * wi
        for j, mulv in enumerate((0.5, -0.5, 1.0)):
            fw.emit("vector", "tensor_scalar", wsc[0:1, 10:11], wsc[0:1, 9:10], mulv, None,
                    op0=ALU.mult, ins=["mw"], outs=["mwj"])
            fw.emit("tensor", "matmul", ctx.psum[1][:, 4 + j:5 + j], hp.view("ones1")[0:1, :],
                    wsc[0:1, 10:11], start=True, stop=True, ins=["ones1", "mwj"], outs=["PB1"])
        fw.emit("vector", "tensor_copy", wsc[:, base:base + 3], ctx.psum[1][:, 4:7],
                ins=["PB1"], outs=["thrs"])
        for c in range(nch):
            g = fw.group()
            fw.dma("sync", g, out=wt[:, 0:width], in_=wd[128 * c:128 * (c + 1), :], outs=["wt0"])
            fw.emit("vector", "tensor_scalar", wb1[:, 0:width], wt[:, 0:width], wsc[:, base:base + 1],
                    None, op0=ALU.is_ge, ins=["wt0", "thrs"], outs=["wb1"])
            fw.emit("vector", "tensor_scalar", wb2[:, 0:width], wt[:, 0:width], wsc[:, base + 1:base + 2],
                    None, op0=ALU.is_le, ins=["wt0", "thrs"], outs=["wb2"])
            fw.emit("vector", "tensor_tensor", wb1[:, 0:width], wb1[:, 0:width], wb2[:, 0:width],
                    op=ALU.subtract, ins=["wb1", "wb2"], outs=["wb1"])
            g2 = fw.group("gpsimd")
            fw.dma("gpsimd", g2, out=wqs[128 * c:128 * (c + 1), :], in_=wb1[:, 0:width],
                   ins=["wb1"], outs=[f"wq{wi}c{c}"])
    hp.free("wt")
    hp.free("wb1")
    hp.free("wb2")

    ccol = hp.alloc("ccol", 96)
    bcol = hp.alloc("bcol", 32)
    g = fw.group()
    fw.dma("sync", g, out=ccol, in_=ctx.din("h_ccol", [128, 96]), outs=["ccol"])
    fw.dma("sync", g, out=bcol, in_=ctx.din("h_bcol", [128, 32]), outs=["bcol"])

    ybuf = hp.alloc("ybuf", NI * 512, subs=tuple(f"yb{i}" for i in range(NI)))
    wq1c = hp.alloc("wq1c", 1024, subs=("wq1c_",))
    wq2c = hp.alloc("wq2c", 1024, subs=("wq2c_",))
    hext = hp.alloc("hext", 516)
    cbuf = hp.alloc("cbuf", 512)
    m2b = hp.alloc("m2b", 512)
    rxbc = hp.alloc("rxbc", 512)
    sybc = hp.alloc("sybc", 512)
    lst = hp.alloc("lst", 64, subs=("lst_",))
    syt = hp.alloc("syt", 16, subs=("syt_m", "syt_u"))
    qq1 = hp.alloc("qq1", 512)
    qq2 = hp.alloc("qq2", 512)
    x3b = hp.alloc("x3b", 2048, subs=("x3b0", "x3b1"))
    ytl = hp.alloc("ytl", 1024)
    lnc = hp.alloc("lnc", 8, subs=("lnc_s", "lnc_q", "lnc_mu", "lnc_rs"))
    fw.emit("gpsimd", "memset", lst, 0.0, outs=["lst_"])
    x4piece = {}

    def x4_write(t, half, src_ap, src_keys):
        b, t4 = divmod(t, 4)
        c = 2 * t4 + half
        dst = hp.view("ixT")[:, 2048 * c + 512 * b:2048 * c + 512 * (b + 1)]
        key = f"x4_{t}_{half}"
        fw.alias(key, ["ixT"])
        fw.emit("vector", "tensor_copy", dst, src_ap, ins=list(src_keys), outs=[key])
        x4piece[(t, half)] = (dst, key)

    d3 = ctx.dram["spill_l1f"]
    for b in range(4):
        grx = fw.group()
        for t4 in range(4):
            col = bass.AP(st.tensor, st.offset + 32 + 4 * b + t4, [[st.ap[0][0], 128], [1, 1]])
            fw.dma("sync", grx, out=ytl[0:1, 128 * t4:128 * (t4 + 1)], in_=col,
                   ins=["qst_r"], outs=["ytl"], allow_slow=True)
        fw.emit("tensor", "matmul", ctx.psum[1][:], hp.view("ones1")[0:1, :], ytl[0:1, 0:512],
                start=True, stop=True, ins=["ones1", "ytl"], outs=["PB1"])
        fw.emit("vector", "tensor_scalar", rxbc, ctx.psum[1][:], wsc[:, 4:5], None, op0=ALU.mult,
                ins=["PB1", "thrs"], outs=["rxbc"])
        if b < 3:
            fw.emit("tensor", "matmul", ctx.psum[1][:, 0:1], hp.view("ones1")[0:1, :],
                    st[0:1, 36 + 4 * b:37 + 4 * b], start=True, stop=True,
                    ins=["ones1", "qst_r"], outs=["PB1"])
            fw.emit("vector", "tensor_scalar", st[:, 48:49], ctx.psum[1][:, 0:1], wsc[:, 4:5], None,
                    op0=ALU.mult, ins=["PB1", "thrs"], outs=["qst_re"])
        fw.emit("gpsimd", "memset", m2b, 0.0, outs=["m2b"])
        for i in range(NI):
            g = fw.group()
            fw.dma("sync", g, out=wq1c,
                   in_=wq1s[:, 128 * i:128 * (i + 1)].rearrange("(c p) f -> p c f", p=128),
                   ins=[f"wq0c{c}" for c in range(NCHUNK)], outs=["wq1c_"])
            fw.mm_group("PB0", [(ctx.psum[0][:], wq1c[:, 128 * c:128 * (c + 1)],
                                 hp.view("ixT")[:, 2048 * c + 512 * b:2048 * c + 512 * (b + 1)])
                                for c in range(NCHUNK)], ins=["wq1c_", "ixT"])
            fw.emit("vector", "tensor_copy", hext[:, 0:1], lst[:, i:i + 1], ins=["lst_"], outs=["hext"])
            fw.emit("vector", "tensor_tensor", hext[:, 1:513], ctx.psum[0][:], rxbc, op=ALU.mult,
                    ins=["PB0", "rxbc"], outs=["hext"])
            if b < 3:
                fw.mm_group("PB1", [(ctx.psum[1][:, 0:1], wq1c[:, 128 * c:128 * (c + 1)],
                                     hp.view("ixT")[:, 2048 * c + 512 * (b + 1):2048 * c + 512 * (b + 1) + 1])
                                    for c in range(NCHUNK)], ins=["wq1c_", "ixT"])
                fw.emit("vector", "tensor_tensor", hext[:, 513:514], ctx.psum[1][:, 0:1], st[:, 48:49],
                        op=ALU.mult, ins=["PB1", "qst_re"], outs=["hext"])
            else:
                fw.emit("gpsimd", "memset", hext[:, 513:514], 0.0, outs=["hext"])
            fw.emit("vector", "tensor_copy", lst[:, i:i + 1], hext[:, 512:513], ins=["hext"], outs=["lst_"])
            fw.emit("vector", "tensor_scalar", cbuf, hext[:, 0:512], ccol[:, i:i + 1], None,
                    op0=ALU.mult, ins=["hext", "ccol"], outs=["cbuf"])
            fw.emit("vector", "scalar_tensor_tensor", cbuf, hext[:, 1:513], ccol[:, 32 + i:33 + i],
                    cbuf, op0=ALU.mult, op1=ALU.add, ins=["hext", "ccol", "cbuf"], outs=["cbuf"])
            fw.emit("vector", "scalar_tensor_tensor", cbuf, hext[:, 2:514], ccol[:, 64 + i:65 + i],
                    cbuf, op0=ALU.mult, op1=ALU.add, ins=["hext", "ccol", "cbuf"], outs=["cbuf"])
            ysl = ybuf[:, 512 * i:512 * (i + 1)]
            fw.emit("scalar", "activation", ysl, cbuf, AF.Silu, bias=bcol[:, i:i + 1], scale=1.0,
                    ins=["cbuf", "bcol"], outs=[f"yb{i}"])
            fw.emit("vector", "tensor_tensor", m2b, m2b, ysl, op=ALU.max,
                    ins=["m2b", f"yb{i}"], outs=["m2b"])
            fw.emit("vector", "scalar_tensor_tensor", m2b, ysl, -1.0, m2b,
                    op0=ALU.mult, op1=ALU.max, ins=["m2b", f"yb{i}"], outs=["m2b"])
        for t4 in range(4):
            fw.emit("tensor", "transpose", ctx.psum[1][:, 0:128], m2b[:, 128 * t4:128 * (t4 + 1)],
                    hp.view("ident"), ins=["m2b", "ident"], outs=["PB1"])
            fw.emit("vector", "tensor_reduce", syt[:, t4:t4 + 1], ctx.psum[1][:, 0:128],
                    axis=AX.X, op=ALU.max, ins=["PB1"], outs=["syt_m"])
        fw.emit("vector", "tensor_scalar", syt[:, 0:4], syt[:, 0:4], 1e-5, None, op0=ALU.max,
                ins=["syt_m"], outs=["syt_m"])
        fw.emit("vector", "reciprocal", syt[:, 4:8], syt[:, 0:4], ins=["syt_m"], outs=["syt_u"])
        fw.emit("vector", "tensor_scalar", syt[:, 4:8], syt[:, 4:8], 127.0, None, op0=ALU.mult,
                ins=["syt_u"], outs=["syt_u"])
        fw.emit("vector", "tensor_scalar", syt[:, 8:12], syt[:, 0:4], 1.0 / 127.0, None, op0=ALU.mult,
                ins=["syt_m"], outs=["syt_m2"])
        fw.emit("vector", "tensor_scalar", syt[:, 12:16], syt[:, 8:12], wsc[:, 7:8], None, op0=ALU.mult,
                ins=["syt_m2", "thrs"], outs=["syt_us"])
        gsy = fw.group()
        for t4 in range(4):
            col = bass.AP(syt.tensor, syt.offset + 4 + t4, [[syt.ap[0][0], 128], [1, 1]])
            fw.dma("sync", gsy, out=ytl[0:1, 128 * t4:128 * (t4 + 1)], in_=col,
                   ins=["syt_u"], outs=["ytl"], allow_slow=True)
        fw.emit("tensor", "matmul", ctx.psum[1][:], hp.view("ones1")[0:1, :], ytl[0:1, 0:512],
                start=True, stop=True, ins=["ones1", "ytl"], outs=["PB1"])
        fw.emit("vector", "tensor_copy", sybc, ctx.psum[1][:], ins=["PB1"], outs=["sybc"])
        for i in range(NI):
            ysl = ybuf[:, 512 * i:512 * (i + 1)]
            fw.emit("vector", "tensor_tensor", ysl, ysl, sybc, op=ALU.mult,
                    ins=[f"yb{i}", "sybc"], outs=[f"yb{i}"])
            emit_round(ctx, ysl, [f"yb{i}"], ysl, f"yb{i}", qq1, qq2, "rqa", "rqb")
        for i in range(NI):
            g = fw.group()
            fw.dma("sync", g, out=wq2c, in_=wq2s[128 * i:128 * (i + 1), :],
                   ins=[f"wq1c{i}"], outs=["wq2c_"])
            for t4 in range(4):
                for half in range(2):
                    if half == 0:
                        po, pk = ctx.psum[t4][:], f"PB{t4}"
                    else:
                        po, pk = ctx.pbig[:, 512 * t4:512 * (t4 + 1)], f"PBIG{t4}"
                    fw.emit("tensor", "matmul", po, ybuf[:, 512 * i + 128 * t4:512 * i + 128 * (t4 + 1)],
                            wq2c[:, 512 * half:512 * (half + 1)],
                            start=(i == 0), stop=(i == NI - 1),
                            ins=[f"yb{i}", "wq2c_"],
                            outs=([pk] if i == 0 else []),
                            extra=([] if i == 0 else [fw.wr.get(pk)]))
                    if i == NI - 1:
                        fw.wr[pk] = ("tensor", fw.cnt["tensor"])
        for t4 in range(4):
            t = 4 * b + t4
            xsl = x3b[:, 1024 * (t % 2):1024 * (t % 2 + 1)]
            xkey = f"x3b{t % 2}"
            fw.dma("sync", fw.group(), out=xsl, in_=d3[128 * t:128 * (t + 1), :],
                   ins=[f"spill_l1f_t{t}"], outs=[xkey])
            for half in range(2):
                if half == 0:
                    po, pk = ctx.psum[t4][:], f"PB{t4}"
                else:
                    po, pk = ctx.pbig[:, 512 * t4:512 * (t4 + 1)], f"PBIG{t4}"
                fw.emit("vector", "tensor_scalar", ytl[:, 512 * half:512 * (half + 1)], po,
                        syt[:, 12 + t4:13 + t4], None, op0=ALU.mult,
                        ins=[pk, "syt_us"], outs=["ytl"])
                fw.emit("vector", "tensor_tensor", ytl[:, 512 * half:512 * (half + 1)],
                        ytl[:, 512 * half:512 * (half + 1)], xsl[:, 512 * half:512 * (half + 1)],
                        op=ALU.add, ins=["ytl", xkey], outs=["ytl"])
            fw.emit("vector", "reduce_sum", lnc[:, 0:1], ytl, axis=AX.X, ins=["ytl"], outs=["lnc_s"])
            fw.emit("scalar", "activation", ybuf[:, 0:1024], ytl, AF.Square,
                    accum_out=lnc[:, 1:2], ins=["ytl"], outs=["yb0", "yb1", "lnc_q"])
            fw.emit("vector", "tensor_scalar", lnc[:, 2:3], lnc[:, 0:1], 1.0 / 1024.0, None,
                    op0=ALU.mult, ins=["lnc_s"], outs=["lnc_mu"])
            fw.emit("vector", "tensor_scalar", lnc[:, 1:2], lnc[:, 1:2], 1.0 / 1024.0, None,
                    op0=ALU.mult, ins=["lnc_q"], outs=["lnc_q"])
            fw.emit("vector", "tensor_tensor", lnc[:, 3:4], lnc[:, 2:3], lnc[:, 2:3], op=ALU.mult,
                    ins=["lnc_mu"], outs=["lnc_rs"])
            fw.emit("vector", "tensor_tensor", lnc[:, 3:4], lnc[:, 1:2], lnc[:, 3:4], op=ALU.subtract,
                    ins=["lnc_q", "lnc_rs"], outs=["lnc_rs"])
            fw.emit("scalar", "activation", lnc[:, 3:4], lnc[:, 3:4], AF.Sqrt,
                    bias=hp.view("epscol")[:, 0:1], scale=1.0, ins=["lnc_rs", "epscol"], outs=["lnc_rs"])
            fw.emit("vector", "reciprocal", lnc[:, 3:4], lnc[:, 3:4], ins=["lnc_rs"], outs=["lnc_rs"])
            for half in range(2):
                fw.emit("vector", "tensor_scalar", ytl[:, 512 * half:512 * (half + 1)],
                        ytl[:, 512 * half:512 * (half + 1)], lnc[:, 2:3], lnc[:, 3:4],
                        op0=ALU.subtract, op1=ALU.mult, ins=["ytl", "lnc_mu", "lnc_rs"], outs=["ytl2"])
                x4_write(t, half, ytl[:, 512 * half:512 * (half + 1)], ["ytl2"])
    ctx.x4piece = x4piece



def build_program(dump=(), blocks=("l0a", "l0f", "l1a", "l1f")):
    nc = bass.Bass("TRN2", target_bir_lowering=False)
    ctx = Ctx(nc, dump=dump)
    load_constants(ctx)
    if "l0a" in blocks:
        build_l0_attn(ctx)
    if "l0f" in blocks:
        build_l0_ffn(ctx)
    if "l1a" in blocks:
        build_l1_attn(ctx)
    if "l1f" in blocks:
        build_l1_ffn(ctx)
    fw, hp = ctx.fw, ctx.heap
    o = ctx.dout("out", [T, D])
    g = fw.group("gpsimd")
    if hasattr(ctx, "x4piece"):
        for t in range(NTILE):
            for half in range(2):
                ap, key = ctx.x4piece[(t, half)]
                fw.dma("gpsimd", g, out=o[128 * t:128 * (t + 1), 512 * half:512 * (half + 1)],
                       in_=ap, ins=[key])
    else:
        src = hp.view("x")
        for t in range(NTILE):
            fw.dma("gpsimd", g, out=o[128 * t:128 * (t + 1), :], in_=src[:, 1024 * t:1024 * (t + 1)], ins=["x"])
    fw.finish("gpsimd")
    return nc, ctx


def kernel(**inputs):
    in_maps = shard_inputs(inputs)
    nc, ctx = build_program()
    res = run_bass_kernel_spmd(nc, in_maps, core_ids=list(range(8)))
    out = np.stack([r["out"] for r in res.results], axis=0)
    return out.astype(np.float32)
